# version 22
# baseline (speedup 1.0000x reference)
import math
import numpy as np
import ml_dtypes
import concourse.bass as bass
import concourse.mybir as mybir
from concourse.bass_utils import run_bass_kernel_spmd
from contextlib import ExitStack

F32 = mybir.dt.float32
BF16 = mybir.dt.bfloat16
AF = mybir.ActivationFunctionType
ALU = mybir.AluOpType
AX = mybir.AxisListType

D = 2048
SBT = 8192
H = 16
TOW = 2048
NE = 8
FF = 1408
FS = 2816
S4 = float(1.0 / math.sqrt(math.sqrt(128.0)))
EPS = 1e-5
NEGBIG = -1.0e30


class Sem:
    def __init__(self, h):
        self.h = h
        self.n = 0


class Phase:

    ENGS = ("sync", "tensor", "vector", "scalar", "gpsimd")

    def __init__(self, nc, name):
        self.nc = nc
        self.name = name
        self.q = {e: [] for e in self.ENGS}

    def op(self, eng, fn, waits=(), inc=None, inc_amt=1, waits_at=()):
        wv = [(s.h, s.n) for s in waits if s.n > 0]
        wv += [(s.h, v) for (s, v) in waits_at if v > 0]
        if inc is not None:
            inc.n += inc_amt

        def run(E, fn=fn, wv=wv, inc=inc, inc_amt=inc_amt):
            for h, v in wv:
                E.wait_ge(h, v)
            inst = fn(E)
            if inc is not None:
                inst.then_inc(inc.h, inc_amt)
        self.q[eng].append(run)

    def play(self):
        with self.nc.Block(self.name) as b:
            for e in self.ENGS:
                ops = self.q[e]
                if not ops:
                    continue
                getattr(b, e)(lambda E, ops=ops: [f(E) for f in ops])


def dma(ph, eng, out, in_, waits=(), inc=None, transpose=False, waits_at=()):
    if transpose:
        ph.op(eng, lambda E: E.dma_start(out=out, in_=in_, transpose=True),
              waits=waits, inc=inc, inc_amt=16, waits_at=waits_at)
    else:
        ph.op(eng, lambda E: E.dma_start(out=out, in_=in_),
              waits=waits, inc=inc, inc_amt=16, waits_at=waits_at)


def build_program(taps=()):
    nc = bass.Bass("TRN2", target_bir_lowering=False)
    tapset = set(taps)

    def dram(name, shape, dt):
        k = "ExternalOutput" if name in tapset else "Internal"
        return nc.dram_tensor(name, shape, dt, kind=k).ap()

    dXq = nc.dram_tensor("Xq", [TOW, D], F32, kind="ExternalInput").ap()
    dXkv = nc.dram_tensor("Xkv", [SBT, D], F32, kind="ExternalInput").ap()
    dWqT = nc.dram_tensor("WqT", [D, D], BF16, kind="ExternalInput").ap()
    dWkT = nc.dram_tensor("WkT", [D, D], BF16, kind="ExternalInput").ap()
    dWvT = nc.dram_tensor("WvT", [D, D], BF16, kind="ExternalInput").ap()
    dWoT = nc.dram_tensor("WoT", [D, D], BF16, kind="ExternalInput").ap()
    dbq = nc.dram_tensor("bq", [1, D], F32, kind="ExternalInput").ap()
    dbk = nc.dram_tensor("bk", [1, D], F32, kind="ExternalInput").ap()
    dbv = nc.dram_tensor("bv", [1, D], F32, kind="ExternalInput").ap()
    dbo = nc.dram_tensor("bo", [1, D], F32, kind="ExternalInput").ap()
    dGWT = nc.dram_tensor("GWT", [D, NE], BF16, kind="ExternalInput").ap()
    dWgT = nc.dram_tensor("WgT", [NE, FF // 128, 128, D], BF16, kind="ExternalInput").ap()
    dWuT = nc.dram_tensor("WuT", [NE, FF // 128, 128, D], BF16, kind="ExternalInput").ap()
    dWdT = nc.dram_tensor("WdT", [NE, FF, D], BF16, kind="ExternalInput").ap()
    dWsgT = nc.dram_tensor("WsgT", [FS // 128, 128, D], BF16, kind="ExternalInput").ap()
    dWsuT = nc.dram_tensor("WsuT", [FS // 128, 128, D], BF16, kind="ExternalInput").ap()
    dWsdT = nc.dram_tensor("WsdT", [FS, D], BF16, kind="ExternalInput").ap()
    dMK = nc.dram_tensor("MK01", [128, 4 * 128], F32, kind="ExternalInput").ap()

    dY = nc.dram_tensor("Yo", [TOW, D], F32, kind="ExternalOutput").ap()

    dh1kv = dram("h1kv", [SBT, D], BF16)
    dh1q = dram("h1q", [TOW, D], BF16)
    dKfm = dram("Kfm", [D, SBT], BF16)
    dVtm = dram("Vtm", [SBT, D], BF16)
    dQfm = dram("Qfm", [D, TOW], BF16)
    dx2 = dram("x2sp", [TOW, D], BF16)
    dh2 = dram("h2rt", [TOW, D], BF16)
    dAtt = dram("attsp", [128, H * TOW], BF16) if "attsp" in tapset else None

    ctx = ExitStack()
    sbuf = lambda n, s, dt: ctx.enter_context(nc.sbuf_tensor(n, s, dt))
    sem = lambda n: Sem(ctx.enter_context(nc.semaphore(n)))

    mm = sem("mm")
    ve = sem("ve")
    se = sem("se")
    ge = sem("ge")
    LX = sem("LX")
    LH = sem("LH")
    LW = sem("LW")
    LK = sem("LK")
    LW2 = sem("LW2")
    TQ = [sem(f"TQ{j}") for j in range(4)]
    SH = [sem(f"SHa{i}") for i in range(4)]
    SH2 = sem("SH2")
    SKV = sem("SKV")
    SX = sem("SX")
    SY = sem("SY")
    all_stores = SH + [SH2, SKV, SX, SY]
    all_sems = [mm, ve, se, ge, LX, LH, LW, LK, LW2] + TQ + all_stores

    def fence(ph):
        for eng in Phase.ENGS:
            ph.op(eng, lambda E: E.nop(), waits=list(all_sems))

    ones_col = sbuf("ones_col", [128, 1], BF16)
    ones_row = sbuf("ones_row", [1, 128], BF16)
    ones_rowf = sbuf("ones_rowf", [1, 128], F32)
    negbig_row = sbuf("negbig_row", [1, 512], BF16)
    onef = sbuf("onef", [1, 1], F32)
    mk01 = sbuf("mk01", [128, 4 * 128], BF16)
    eye8 = sbuf("eye8", [8, 8], BF16)
    eye8f = sbuf("eye8f", [8, 8], F32)
    bqc = sbuf("bqc", [128, H], F32)
    bkc = sbuf("bkc", [128, H], F32)
    bv_bf = sbuf("bv_bf", [1, D], BF16)
    bo_bf = sbuf("bo_bf", [1, D], BF16)

    ph = Phase(nc, "p0")
    with ExitStack() as pctx:
        pc0 = pctx.enter_context(nc.psum_tensor("pc0", [128, 1], F32))
        brow = pctx.enter_context(nc.sbuf_tensor("brow", [1, D], F32))
        brow2 = pctx.enter_context(nc.sbuf_tensor("brow2", [1, D], F32))
        dma(ph, "sync", brow[:, :], dbq, inc=LX)
        dma(ph, "sync", brow2[:, :], dbk, inc=LX)
        dma(ph, "gpsimd", bv_bf[:, :], dbv, inc=LW)
        dma(ph, "gpsimd", bo_bf[:, :], dbo, inc=LW)
        dma(ph, "gpsimd", mk01[:, :], dMK, inc=LW)
        ph.op("gpsimd", lambda E: E.iota(eye8f[:, :], pattern=[[1, 8]], base=0,
                                         channel_multiplier=-1,
                                         allow_small_or_imprecise_dtypes=True), inc=ge)
        ph.op("vector", lambda E: E.memset(ones_col[:, :], 1.0), inc=ve)
        ph.op("vector", lambda E: E.memset(ones_row[:, :], 1.0), inc=ve)
        ph.op("vector", lambda E: E.memset(ones_rowf[:, :], 1.0), inc=ve)
        ph.op("vector", lambda E: E.memset(negbig_row[:, :], NEGBIG), inc=ve)
        ph.op("vector", lambda E: E.memset(onef[:, :], 1.0), inc=ve)
        ph.op("vector", lambda E: E.tensor_scalar(eye8f[:, :], eye8f[:, :], 0.0, None,
                                                  ALU.is_equal), waits=[ge], inc=ve)
        ph.op("vector", lambda E: E.tensor_copy(eye8[:, :], eye8f[:, :]),
              waits=[ve], inc=ve)
        ph.op("vector", lambda E: E.tensor_scalar(brow[:, :], brow[:, :], S4, None,
                                                  ALU.mult), waits=[LX], inc=ve)
        ph.op("vector", lambda E: E.tensor_scalar(brow2[:, :], brow2[:, :], S4, None,
                                                  ALU.mult), waits=[LX], inc=ve)
        for h in range(H):
            ph.op("tensor", lambda E, h=h: E.matmul(pc0[:, :],
                                                    brow[0:1, 128 * h:128 * (h + 1)],
                                                    onef[0:1, 0:1], start=True, stop=True),
                  waits=[ve], inc=mm)
            ph.op("vector", lambda E, h=h: E.tensor_copy(bqc[:, h:h + 1], pc0[:, :]),
                  waits=[mm], inc=ve)
            ph.op("tensor", lambda E, h=h: E.matmul(pc0[:, :],
                                                    brow2[0:1, 128 * h:128 * (h + 1)],
                                                    onef[0:1, 0:1], start=True, stop=True),
                  waits=[ve], inc=mm)
            ph.op("vector", lambda E, h=h: E.tensor_copy(bkc[:, h:h + 1], pc0[:, :]),
                  waits=[mm], inc=ve)
        ph.play()

    def load_w_strips(ph, wdram, dst, waits=(), waits_at=()):
        lw_strips = []
        for ic in range(16):
            dma(ph, "gpsimd", dst[:, D * ic:D * (ic + 1)],
                wdram[128 * ic:128 * (ic + 1), :], waits=waits,
                waits_at=waits_at, inc=LW)
            lw_strips.append(LW.n)
        return lw_strips

    ph = Phase(nc, "pab")
    fence(ph)
    with ExitStack() as pctx:
        psb = lambda n, s, dt: pctx.enter_context(nc.sbuf_tensor(n, s, dt))
        pps = lambda n, s: pctx.enter_context(nc.psum_tensor(n, s, F32))
        xin = [psb(f"pa_x{j}", [128, D], F32) for j in range(2)]
        cen = [psb(f"pa_c{j}", [128, D], F32) for j in range(2)]
        sq = [psb("pa_s0", [128, D], BF16)] * 2
        st1 = [psb(f"pa_m{j}", [128, 2], F32) for j in range(2)]
        st2 = [psb(f"pa_v{j}", [128, 4], F32) for j in range(2)]
        h1o = [psb(f"pa_o{j}", [128, D], BF16) for j in range(2)]
        wbig = psb("b_w", [128, 16 * D], BF16)
        h1f = psb("b_h", [128, 16 * D], BF16)
        est = [psb(f"b_e{j}", [128, D], BF16) for j in range(2)]
        pk = [pps(f"b_p{j}", [128, 2048]) for j in range(2)]

        ln_ve = {}
        ln_store_g = {}
        ln_sq_se = {}

        def emit_ln_tile(i):
            if i < 64:
                src, dst, ssem = dXkv[128 * i:128 * (i + 1), :], \
                    dh1kv[128 * i:128 * (i + 1), :], SH[i // 16]
            else:
                j = i - 64
                src, dst, ssem = dXq[128 * j:128 * (j + 1), :], \
                    dh1q[128 * j:128 * (j + 1), :], SH2
            sl = i % 2
            dma(ph, "scalar", xin[sl][:, :], src,
                waits=([ve] if i < 2 else []),
                waits_at=([(ve, ln_ve[i - 2])] if i >= 2 else ()), inc=LX)
            ph.op("vector", lambda E, sl=sl: E.reduce_sum(st1[sl][:, 0:1],
                                                          xin[sl][:, :], AX.X),
                  waits=[LX], inc=ve)
            ph.op("vector", lambda E, sl=sl: E.tensor_scalar(st1[sl][:, 1:2],
                                                             st1[sl][:, 0:1],
                                                             1.0 / D, None, ALU.mult),
                  waits=[ve], inc=ve)
            ph.op("vector", lambda E, sl=sl: E.tensor_scalar(cen[sl][:, :], xin[sl][:, :],
                                                             st1[sl][:, 1:2], None,
                                                             ALU.subtract),
                  waits=[ve],
                  waits_at=([(se, ln_sq_se[i - 2])] if i >= 2 else ()), inc=ve)
            ph.op("scalar", lambda E, sl=sl: E.activation(sq[sl][:, :], cen[sl][:, :],
                                                          AF.Square,
                                                          accum_out=st2[sl][:, 0:1]),
                  waits=[ve], inc=se)
            ln_sq_se[i] = se.n
            ph.op("vector", lambda E, sl=sl: E.tensor_scalar(st2[sl][:, 1:2],
                                                             st2[sl][:, 0:1],
                                                             1.0 / D, EPS, ALU.mult,
                                                             ALU.add),
                  waits=[se], inc=ve)
            ph.op("scalar", lambda E, sl=sl: E.activation(st2[sl][:, 2:3], st2[sl][:, 1:2],
                                                          AF.Sqrt), waits=[ve], inc=se)
            ph.op("vector", lambda E, sl=sl: E.reciprocal(st2[sl][:, 3:4], st2[sl][:, 2:3]),
                  waits=[se], inc=ve)
            ph.op("vector", lambda E, sl=sl: E.tensor_scalar(h1o[sl][:, :], cen[sl][:, :],
                                                             st2[sl][:, 3:4], None,
                                                             ALU.mult),
                  waits=([ve] if i < 2 else [ve]),
                  waits_at=([ln_store_g[i - 2]] if i >= 2 else ()), inc=ve)
            dma(ph, "gpsimd", dst, h1o[sl][:, :], waits=[ve], inc=ssem)
            ln_store_g[i] = (ssem, ssem.n)
            ln_ve[i] = ve.n

        lw_s = load_w_strips(ph, dWkT, wbig, waits=[mm])
        for i in range(16):
            emit_ln_tile(i)
        ln_next = 16

        se_ev = {}
        skv_g = {}
        uu = 0

        def evac_store(pp, fn, dst):
            nonlocal uu
            ph.op("scalar", fn,
                  waits=[mm],
                  waits_at=([(SKV, skv_g[uu - 2])] if uu >= 2 else ()), inc=se)
            se_ev[uu] = se.n
            dma(ph, "gpsimd", dst, est[pp][:, :], waits=[se], inc=SKV)
            skv_g[uu] = SKV.n
            uu += 1

        def quarter_transposes(src, tile0, ssem, tb, sec_end, qsec=None):
            lh_qq = []
            for qq in range(4):
                gv = ln_store_g[tile0 + 4 * qq + 3]
                se2 = qsec[qq] if qsec is not None else sec_end
                for ic in range(16):
                    dma(ph, "sync",
                        h1f[:, D * ic + 512 * qq:D * ic + 512 * (qq + 1)],
                        src[512 * qq:512 * (qq + 1), 128 * ic:128 * (ic + 1)],
                        waits_at=[gv] + ([(mm, se2)]
                                         if se2 is not None else []),
                        inc=TQ[qq], transpose=True)
                lh_qq.append((TQ[qq], TQ[qq].n))
            return lh_qq

        sec_end = None
        for tb in range(4):
            lh_qq = quarter_transposes(dh1kv[2048 * tb:2048 * (tb + 1), :],
                                       16 * tb, SH[tb], tb, sec_end)
            for h in range(H):
                pp = h % 2
                w_at0 = [(LW, lw_s[15])]
                if uu >= 2:
                    w_at0.append((se, se_ev[uu - 2]))
                for ic in range(16):
                    for q in range(4):
                        w_at = []
                        if ic == 0 and q == 0:
                            w_at += w_at0
                        if ic == 0 and h == 0:
                            w_at.append(lh_qq[q])
                        ph.op("tensor", lambda E, pp=pp, ic=ic, q=q, h=h: E.matmul(
                            pk[pp][:, 512 * q:512 * (q + 1)],
                            wbig[:, D * ic + 128 * h:D * ic + 128 * (h + 1)],
                            h1f[:, D * ic + 512 * q:D * ic + 512 * (q + 1)],
                            start=(ic == 0), stop=(ic == 15)),
                            waits_at=w_at,
                            inc=(mm if (ic == 15 and q == 3) else None))
                evac_store(pp, lambda E, pp=pp, h=h: E.activation(
                    est[pp][:, :], pk[pp][:, :], AF.Identity,
                    bias=bkc[:, h:h + 1], scale=S4),
                    dKfm[128 * h:128 * (h + 1), 2048 * tb:2048 * (tb + 1)])
                if ln_next < 80:
                    emit_ln_tile(ln_next)
                    ln_next += 1
            sec_end = mm.n

        lw_s = load_w_strips(ph, dWvT, wbig, waits_at=[(mm, sec_end)])
        vq_prev = None
        for tb in range(4):
            lh_qq = quarter_transposes(dh1kv[2048 * tb:2048 * (tb + 1), :],
                                       16 * tb, SH[tb], tb, sec_end,
                                       qsec=vq_prev)
            vq_cur = [None] * 4
            for tt in range(16):
                pp = tt % 2
                w_at0 = []
                if uu >= 2:
                    w_at0.append((se, se_ev[uu - 2]))
                for ic in range(16):
                    for q in range(4):
                        w_at = []
                        if ic == 0 and q == 0:
                            w_at += w_at0
                            w_at.append(lh_qq[tt // 4])
                        if q == 0 and tb == 0 and tt == 0:
                            w_at.append((LW, lw_s[ic]))
                        ph.op("tensor", lambda E, pp=pp, ic=ic, q=q, tt=tt: E.matmul(
                            pk[pp][:, 512 * q:512 * (q + 1)],
                            h1f[:, D * ic + 128 * tt:D * ic + 128 * (tt + 1)],
                            wbig[:, D * ic + 512 * q:D * ic + 512 * (q + 1)],
                            start=(ic == 0), stop=False),
                            waits_at=w_at)
                for q in range(4):
                    ph.op("tensor", lambda E, pp=pp, q=q: E.matmul(
                        pk[pp][:, 512 * q:512 * (q + 1)],
                        ones_row[0:1, :], bv_bf[0:1, 512 * q:512 * (q + 1)],
                        start=False, stop=True),
                        inc=(mm if q == 3 else None))
                if tt % 4 == 3:
                    vq_cur[tt // 4] = mm.n
                evac_store(pp, lambda E, pp=pp: E.activation(
                    est[pp][:, :], pk[pp][:, :], AF.Copy),
                    dVtm[2048 * tb + 128 * tt:2048 * tb + 128 * (tt + 1), :])
                if ln_next < 80:
                    emit_ln_tile(ln_next)
                    ln_next += 1
            vq_prev = vq_cur
            sec_end = mm.n

        lw_s = load_w_strips(ph, dWqT, wbig, waits_at=[(mm, sec_end)])
        lh_qq = quarter_transposes(dh1q, 64, SH2, 0, sec_end)
        for h in range(H):
            pp = h % 2
            w_at0 = []
            if uu >= 2:
                w_at0.append((se, se_ev[uu - 2]))
            for ic in range(16):
                for q in range(4):
                    w_at = []
                    if ic == 0 and q == 0:
                        w_at += w_at0
                    if ic == 0 and h == 0:
                        w_at.append(lh_qq[q])
                    if q == 0 and h == 0:
                        w_at.append((LW, lw_s[ic]))
                    ph.op("tensor", lambda E, pp=pp, ic=ic, q=q, h=h: E.matmul(
                        pk[pp][:, 512 * q:512 * (q + 1)],
                        wbig[:, D * ic + 128 * h:D * ic + 128 * (h + 1)],
                        h1f[:, D * ic + 512 * q:D * ic + 512 * (q + 1)],
                        start=(ic == 0), stop=(ic == 15)),
                        waits_at=w_at,
                        inc=(mm if (ic == 15 and q == 3) else None))
            evac_store(pp, lambda E, pp=pp, h=h: E.activation(
                est[pp][:, :], pk[pp][:, :], AF.Identity,
                bias=bqc[:, h:h + 1], scale=S4),
                dQfm[128 * h:128 * (h + 1), :])
        ph.play()

    att_ctx = ExitStack()
    att = att_ctx.enter_context(nc.sbuf_tensor("att", [128, H * TOW], BF16))

    ph = Phase(nc, "pc")
    fence(ph)
    with ExitStack() as pctx:
        psb = lambda n, s, dt: pctx.enter_context(nc.sbuf_tensor(n, s, dt))
        pps = lambda n, s: pctx.enter_context(nc.psum_tensor(n, s, F32))
        kf = [psb(f"c_k{j}", [128, SBT], BF16) for j in range(2)]
        vf = [psb(f"c_v{j}", [128, SBT], BF16) for j in range(2)]
        qf = [psb(f"c_q{j}", [128, TOW], BF16) for j in range(2)]
        sqs = [psb(f"c_sqs{j}", [128, SBT + TOW], BF16) for j in range(2)]
        mxq = psb("c_mq", [1, 2], F32)
        mxk = psb("c_mk", [1, 2], F32)
        bsq = psb("c_bsq", [1, 2], F32)
        bneg = psb("c_bn", [1, 2], F32)
        bcol = [psb(f"c_bc{j}", [128, 1], F32) for j in range(2)]
        PT = [psb(f"c_pt{j}", [128, 1024], BF16) for j in range(4)]
        lrow = psb("c_lr", [1, 512], F32)
        rl = psb("c_rl", [1, 512], F32)
        bcsb = psb("c_bcs", [128, 512], F32)
        pST = [pps(f"c_ps{j}", [128, 1024]) for j in range(2)]
        pAV2 = [pps(f"c_pav{j}", [128, 512]) for j in range(2)]
        pl = pps("c_pl", [1, 512])
        pbc = pps("c_pbc", [128, 512])

        accq = psb("c_aq", [1, 512], F32)
        acck = psb("c_ak", [1, 512], F32)
        def emit_bounds_sq(hh):
            hhs = hh % 2
            sq2 = sqs[hhs]
            ph.op("vector", lambda E, hhs=hhs, sq2=sq2: E.tensor_tensor(
                sq2[:, SBT:SBT + TOW], qf[hhs][:, :], qf[hhs][:, :], ALU.mult),
                waits_at=[(LK, lk_after[hh]), (mm, bounds_mm.get(hh - 2, 0))], inc=ve)
            ph.op("vector", lambda E, hhs=hhs, sq2=sq2: E.tensor_tensor(
                sq2[:, 0:SBT], kf[hhs][:, :], kf[hhs][:, :], ALU.mult),
                waits_at=[(LK, lk_after[hh]), (mm, bounds_mm.get(hh - 2, 0))], inc=ve)

        def emit_bounds(hh):
            hhs = hh % 2
            sq2 = sqs[hhs]
            nbank = [pbc[0:1, :], pl[0:1, 0:512]]

            def normmax(acc, nch, base=SBT):
                for j in range(nch):
                    bk = nbank[j % 2]
                    ph.op("tensor", lambda E, j=j, bk=bk, sq2=sq2, base=base: E.matmul(
                        bk, ones_col[:, 0:1],
                        sq2[:, base + 512 * j:base + 512 * (j + 1)],
                        start=True, stop=True),
                        waits=[ve], inc=mm)
                    if j == 0:
                        ph.op("vector", lambda E, bk=bk, acc=acc: E.tensor_copy(
                            acc[:, :], bk), waits=[mm], inc=ve)
                    else:
                        ph.op("vector", lambda E, bk=bk, acc=acc: E.tensor_tensor(
                            acc[:, :], acc[:, :], bk, ALU.max),
                            waits=[mm], inc=ve)

            def normmax2(acc, nch):
                normmax(acc, nch, base=0)
            normmax(accq, TOW // 512)
            normmax2(acck, SBT // 512)
            ph.op("vector", lambda E, hhs=hhs: E.tensor_reduce(
                mxq[0:1, hhs:hhs + 1], accq[:, :], AX.X, ALU.max),
                waits=[ve], inc=ve)
            ph.op("vector", lambda E, hhs=hhs: E.tensor_reduce(
                mxk[0:1, hhs:hhs + 1], acck[:, :], AX.X, ALU.max),
                waits=[ve], inc=ve)
            ph.op("vector", lambda E, hhs=hhs: E.tensor_tensor(
                bsq[0:1, hhs:hhs + 1], mxq[0:1, hhs:hhs + 1], mxk[0:1, hhs:hhs + 1],
                ALU.mult), waits=[ve], inc=ve)
            ph.op("scalar", lambda E, hhs=hhs: E.activation(
                bsq[0:1, hhs:hhs + 1], bsq[0:1, hhs:hhs + 1], AF.Sqrt),
                waits=[ve], inc=se)
            ph.op("vector", lambda E, hhs=hhs: E.tensor_scalar(
                bneg[0:1, hhs:hhs + 1], bsq[0:1, hhs:hhs + 1], -1.0, None, ALU.mult),
                waits=[se], inc=ve)
            ph.op("tensor", lambda E, hhs=hhs: E.matmul(
                pbc[:, 0:1], ones_rowf[0:1, :], bneg[0:1, hhs:hhs + 1],
                start=True, stop=True),
                waits=[ve], inc=mm)
            ph.op("vector", lambda E, hhs=hhs: E.tensor_copy(
                bcol[hhs][:, :], pbc[:, 0:1]),
                waits=[mm, se], inc=ve)

        hd_end = {}
        lk_after = {}
        bounds_mm = {}
        attmul_ve = {}
        lrowcp_ve = {}

        def head_loads(h):
            hs = h % 2
            w_at = [(mm, hd_end.get(h - 2, (0, 0))[0]),
                    (se, hd_end.get(h - 2, (0, 0))[1])]
            dma(ph, "sync", kf[hs][:, :], dKfm[128 * h:128 * (h + 1), :],
                waits=[SKV], waits_at=w_at, inc=LK)
            dma(ph, "sync", qf[hs][:, :], dQfm[128 * h:128 * (h + 1), :],
                inc=LK)
            dma(ph, "sync",
                vf[hs][:, :].rearrange("p (a d) -> p a d", d=128),
                dVtm[:, 128 * h:128 * (h + 1)].rearrange("(a p) d -> p a d", p=128),
                inc=LK)
            lk_after[h] = LK.n
        head_loads(0)
        head_loads(1)
        for h in range(H):
            hs = h % 2
            hh_cur = h
            head_start_se = se.n
            if h == 0:
                emit_bounds_sq(0)
                emit_bounds(0)
                bounds_mm[0] = mm.n

            bcol_ve = ve.n
            seq = []
            gidx = 0

            def emit_ST(t, rec):
                sl = t % 2
                c, u, m = rec["c"], rec["u"], rec["m"]
                if t >= 2:
                    w_at = [(se, seq[t - 2]["se_post"])]
                else:
                    w_at = [(se, head_start_se), (ve, bcol_ve)]
                if rec["hc"] == 0:
                    w_at = w_at + [(LK, lk_after[hh_cur])]
                w_first = []
                for kki in range(2):
                    kk = 2 * rec["half"] + kki
                    seg = 512 * kki
                    if u > 0:
                        ph.op("tensor", lambda E, sl=sl, seg=seg, u=u: E.matmul(
                            pST[sl][:, seg:seg + 128 * u],
                            ones_row[0:1, :], negbig_row[0:1, 0:128 * u],
                            start=True, stop=True),
                            waits=(w_first if kki == 0 else ()),
                            waits_at=(w_at if kki == 0 else ()))
                        w_st, w_sat = (), ()
                    else:
                        w_st = w_first if kki == 0 else ()
                        w_sat = w_at if kki == 0 else ()
                    ph.op("tensor", lambda E, sl=sl, seg=seg, u=u, c=c, kk=kk,
                          m=m, hs=hs: E.matmul(
                        pST[sl][:, seg + 128 * u:seg + 512],
                        kf[hs][:, 128 * (4 * c + kk):128 * (4 * c + kk + 1)],
                        qf[hs][:, 512 * m + 128 * u:512 * (m + 1)],
                        start=True, stop=True),
                        waits=w_st, waits_at=w_sat,
                        inc=(mm if kki == 1 else None))
                rec["mm_st"] = mm.n

            def emit_exp(t, rec):
                sl = t % 2
                pt = PT[t % 4]
                w_at = [(mm, rec["mm_st"]), (ve, bcol_ve)]
                ph.op("scalar", lambda E, sl=sl, pt=pt, bc=bcol[hh_cur % 2]: E.activation(
                    pt[:, :], pST[sl][:, :], AF.Exp, bias=bc[:, 0:1]),
                    waits_at=w_at, inc=se)
                rec["se_post"] = se.n
                if rec["diag"]:
                    for kki in range(2):
                        kk = 2 * rec["half"] + kki
                        seg = 512 * kki
                        ph.op("vector", lambda E, pt=pt, seg=seg, ud=rec["ud"],
                              kk=kk: E.tensor_tensor(
                            pt[:, seg + 128 * ud:seg + 128 * (ud + 1)],
                            pt[:, seg + 128 * ud:seg + 128 * (ud + 1)],
                            mk01[:, 128 * kk:128 * (kk + 1)], ALU.mult),
                            waits=[LW], waits_at=[(se, rec["se_post"])], inc=ve)
                rec["ve_post"] = ve.n

            def emit_AV(t, rec):
                pt = PT[t % 4]
                first, last = rec["first"], rec["last"]
                g = 4 * hh_cur + rec["m"]
                pav = pAV2[g % 2]
                w_at = [(se, rec["se_post"])]
                if rec["diag"]:
                    w_at.append((ve, rec["ve_post"]))
                for kki in range(2):
                    kk = 2 * rec["half"] + kki
                    seg = 512 * kki
                    av_first = (first and kki == 0)
                    av_last = (last and kki == 1)
                    ph.op("tensor", lambda E, pt=pt, seg=seg, c=rec["c"], kk=kk,
                          hs=hs, af=av_first, al=av_last, pav=pav: E.matmul(
                        pav[:, :],
                        vf[hs][:, 128 * (4 * c + kk):128 * (4 * c + kk + 1)],
                        pt[:, seg:seg + 512],
                        start=af, stop=al),
                        waits_at=(w_at + [(ve, attmul_ve.get(g - 2, 0))]
                                  if av_first else (w_at if kki == 0 else ())),
                        inc=(mm if kki == 1 else None))
                rec["av_mm"] = mm.n

            def emit_L(t, rec):
                pt = PT[t % 4]
                first, last = rec["first"], rec["last"]
                g = 4 * hh_cur + rec["m"]
                w_at = [(se, rec["se_post"])]
                if rec["diag"]:
                    w_at.append((ve, rec["ve_post"]))
                if first:
                    w_at.append((ve, lrowcp_ve.get(g - 1, 0)))
                ph.op("tensor", lambda E, pt=pt, first=first: E.matmul(
                    pl[:, 0:512], ones_col[:, 0:1], pt[:, 0:512],
                    start=first, stop=False),
                    waits_at=w_at)
                ph.op("tensor", lambda E, pt=pt, last=last: E.matmul(
                    pl[:, 0:512], ones_col[:, 0:1], pt[:, 512:1024],
                    start=False, stop=last),
                    inc=mm)
                rec["l_mm"] = mm.n

            def emit_finalize_a(m, tE):
                g = 4 * h + m
                ph.op("vector", lambda E: E.tensor_copy(lrow[0:1, 0:512], pl[:, :]),
                      waits_at=[(mm, seq[tE]["l_mm"])], inc=ve)
                lrowcp_ve[g] = ve.n
                ph.op("vector", lambda E: E.reciprocal(rl[:, :], lrow[0:1, 0:512]),
                      waits=[ve], inc=ve)
                return ve.n

            def emit_finalize_b(m, rec_ve):
                g = 4 * h + m
                ph.op("tensor", lambda E: E.matmul(pbc[:, :], ones_rowf[0:1, :],
                                                   rl[0:1, :], start=True, stop=True),
                      waits_at=[(ve, rec_ve)], inc=mm)
                ph.op("vector", lambda E: E.tensor_copy(bcsb[:, :], pbc[:, :]),
                      waits=[mm], inc=ve)
                ph.op("vector", lambda E, h=h, m=m, g=g: E.tensor_tensor(
                    att[:, TOW * h + 512 * m:TOW * h + 512 * (m + 1)],
                    pAV2[g % 2][:, :], bcsb[:, :], ALU.mult),
                    waits=[ve], inc=ve)
                attmul_ve[g] = ve.n
                if m == 0 and h + 1 < H:
                    emit_bounds_sq(h + 1)
                if m == 1 and h + 1 < H:
                    emit_bounds(h + 1)
                    bounds_mm[h + 1] = mm.n

            fin_at = {}
            finb_at = {}
            for m in range(4):
                nch = 4 * m + 4
                nhalf = 2 * nch
                for hc in range(nhalf):
                    c = hc // 2
                    half = hc % 2
                    u = c - 4 * m if c > 4 * m else 0
                    rec = {"c": c, "half": half, "u": u, "m": m, "hc": hc,
                           "diag": (c >= 4 * m), "ud": c - 4 * m,
                           "first": (hc == 0), "last": (hc == nhalf - 1)}
                    seq.append(rec)
                    t = gidx
                    gidx += 1
                    emit_ST(t, rec)
                    emit_exp(t, rec)
                    if t >= 2:
                        emit_AV(t - 2, seq[t - 2])
                    if t in finb_at:
                        mf, rv = finb_at.pop(t)
                        emit_finalize_b(mf, rv)
                    if t >= 3:
                        emit_L(t - 3, seq[t - 3])
                        if (t - 3) in fin_at:
                            mf, tE = fin_at.pop(t - 3)
                            finb_at[t + 1] = (mf, emit_finalize_a(mf, tE))
                fin_at[gidx - 1] = (m, gidx - 1)
            emit_AV(gidx - 2, seq[gidx - 2])
            emit_AV(gidx - 1, seq[gidx - 1])
            for tf in (gidx - 3, gidx - 2, gidx - 1):
                emit_L(tf, seq[tf])
                if tf in fin_at:
                    mf, tE = fin_at.pop(tf)
                    emit_finalize_b(mf, emit_finalize_a(mf, tE))
            for k2 in list(finb_at):
                mf, rv = finb_at.pop(k2)
                emit_finalize_b(mf, rv)
            hd_end[h] = (mm.n, se.n)
            if h + 2 < H:
                head_loads(h + 2)
        ph.play()

    if "attsp" in tapset:
        ph = Phase(nc, "ptap")
        dma(ph, "sync", dAtt[:, :], att[:, :], waits=[ve], inc=SX)
        ph.play()

    ph = Phase(nc, "pd")
    fence(ph)
    with ExitStack() as pctx:
        psb = lambda n, s, dt: pctx.enter_context(nc.sbuf_tensor(n, s, dt))
        pps = lambda n, s: pctx.enter_context(nc.psum_tensor(n, s, F32))
        wbig = psb("d_w", [128, 16 * D], BF16)
        po = [pps(f"d_p{j}", [128, 2048]) for j in range(2)]
        xq2 = [psb(f"d_x{j}", [128, D], F32) for j in range(2)]
        cen = [psb(f"d_c{j}", [128, D], F32) for j in range(2)]
        sq = [psb("d_s0", [128, D], BF16)] * 2
        st1 = [psb(f"d_m{j}", [128, 4], F32) for j in range(2)]
        st2 = [psb(f"d_v{j}", [128, 4], F32) for j in range(2)]
        x2f = [psb("d_20", [128, D], F32)] * 2
        ob = [psb(f"d_b{j}", [128, D], BF16) for j in range(2)]
        load_w_strips(ph, dWoT, wbig, waits=[mm])
        se_evac = {}
        ve_x2add = {}
        ve_obmul = {}
        for tt in range(16):
            pp = tt % 2
            dma(ph, "sync", xq2[pp][:, :], dXq[128 * tt:128 * (tt + 1), :],
                waits=([ve] if tt < 2 else []),
                waits_at=([(ve, ve_x2add[tt - 2])] if tt >= 2 else ()), inc=LX)
            for ic in range(16):
                for q in range(4):
                    ph.op("tensor", lambda E, pp=pp, ic=ic, q=q, tt=tt: E.matmul(
                        po[pp][:, 512 * q:512 * (q + 1)],
                        att[:, TOW * ic + 128 * tt:TOW * ic + 128 * (tt + 1)],
                        wbig[:, D * ic + 512 * q:D * ic + 512 * (q + 1)],
                        start=(ic == 0), stop=False),
                        waits=(([LW, ve, se] if tt < 2 else [LW])
                               if (ic == 0 and q == 0) else ()),
                        waits_at=([(se, se_evac[tt - 2])]
                                  if (ic == 0 and q == 0 and tt >= 2) else ()))
            for q in range(4):
                ph.op("tensor", lambda E, pp=pp, q=q: E.matmul(
                    po[pp][:, 512 * q:512 * (q + 1)],
                    ones_row[0:1, :], bo_bf[0:1, 512 * q:512 * (q + 1)],
                    start=False, stop=True),
                    waits=([LW] if q == 0 else ()),
                    inc=(mm if q == 3 else None))
            ph.op("scalar", lambda E, pp=pp: E.activation(cen[pp][:, :], po[pp][:, :],
                                                          AF.Copy),
                  waits=([mm, ve] if tt < 2 else [mm]),
                  waits_at=([(ve, ve_obmul[tt - 2])] if tt >= 2 else ()), inc=se)
            se_evac[tt] = se.n
            ph.op("vector", lambda E, pp=pp: E.reduce_sum(st1[pp][:, 0:1], cen[pp][:, :],
                                                          AX.X), waits=[se], inc=ve)
            ph.op("vector", lambda E, pp=pp: E.tensor_scalar(st1[pp][:, 1:2],
                                                             st1[pp][:, 0:1],
                                                             -1.0 / D, None, ALU.mult),
                  waits=[ve], inc=ve)
            ph.op("scalar", lambda E, pp=pp: E.activation(cen[pp][:, :], cen[pp][:, :],
                                                          AF.Identity,
                                                          bias=st1[pp][:, 1:2]),
                  waits=[ve], inc=se)
            ph.op("scalar", lambda E, pp=pp: E.activation(sq[pp][:, :], cen[pp][:, :],
                                                          AF.Square,
                                                          accum_out=st2[pp][:, 0:1]),
                  waits=[ve, se], inc=se)
            ph.op("vector", lambda E, pp=pp: E.tensor_scalar(st2[pp][:, 1:2],
                                                             st2[pp][:, 0:1],
                                                             1.0 / D, EPS, ALU.mult,
                                                             ALU.add),
                  waits=[se], inc=ve)
            ph.op("scalar", lambda E, pp=pp: E.activation(st2[pp][:, 2:3], st2[pp][:, 1:2],
                                                          AF.Sqrt), waits=[ve], inc=se)
            ph.op("vector", lambda E, pp=pp: E.reciprocal(st2[pp][:, 3:4], st2[pp][:, 2:3]),
                  waits=[se], inc=ve)
            ph.op("vector", lambda E, pp=pp: E.tensor_scalar(cen[pp][:, :], cen[pp][:, :],
                                                             st2[pp][:, 3:4], None,
                                                             ALU.mult),
                  waits=[ve], inc=ve)
            ph.op("vector", lambda E, pp=pp: E.tensor_tensor(x2f[pp][:, :], xq2[pp][:, :],
                                                             cen[pp][:, :], ALU.add),
                  waits=[ve, LX], inc=ve)
            ve_x2add[tt] = ve.n
            ph.op("scalar", lambda E, pp=pp: E.activation(ob[pp][:, :], x2f[pp][:, :],
                                                          AF.Copy),
                  waits=[ve, SX], inc=se)
            dma(ph, "sync", dx2[128 * tt:128 * (tt + 1), :], ob[pp][:, :],
                waits=[se], inc=SX)
            ph.op("vector", lambda E, pp=pp: E.reduce_sum(st1[pp][:, 2:3], x2f[pp][:, :],
                                                          AX.X), waits=[ve], inc=ve)
            ph.op("vector", lambda E, pp=pp: E.tensor_scalar(st1[pp][:, 3:4],
                                                             st1[pp][:, 2:3],
                                                             -1.0 / D, None, ALU.mult),
                  waits=[ve], inc=ve)
            ph.op("scalar", lambda E, pp=pp: E.activation(cen[pp][:, :], x2f[pp][:, :],
                                                          AF.Identity,
                                                          bias=st1[pp][:, 3:4]),
                  waits=[ve], inc=se)
            ph.op("scalar", lambda E, pp=pp: E.activation(sq[pp][:, :], cen[pp][:, :],
                                                          AF.Square,
                                                          accum_out=st2[pp][:, 0:1]),
                  waits=[ve, se], inc=se)
            ph.op("vector", lambda E, pp=pp: E.tensor_scalar(st2[pp][:, 1:2],
                                                             st2[pp][:, 0:1],
                                                             1.0 / D, EPS, ALU.mult,
                                                             ALU.add),
                  waits=[se], inc=ve)
            ph.op("scalar", lambda E, pp=pp: E.activation(st2[pp][:, 2:3], st2[pp][:, 1:2],
                                                          AF.Sqrt), waits=[ve], inc=se)
            ph.op("vector", lambda E, pp=pp: E.reciprocal(st2[pp][:, 3:4], st2[pp][:, 2:3]),
                  waits=[se], inc=ve)
            ph.op("vector", lambda E, pp=pp: E.tensor_scalar(ob[pp][:, :], cen[pp][:, :],
                                                             st2[pp][:, 3:4], None,
                                                             ALU.mult),
                  waits=[ve, se, SX], inc=ve)
            ve_obmul[tt] = ve.n
            dma(ph, "sync", dh2[128 * tt:128 * (tt + 1), :], ob[pp][:, :],
                waits=[ve], inc=SX)
        ph.play()
    att_ctx.close()

    ph = Phase(nc, "pe")
    fence(ph)
    with ExitStack() as pctx:
        psb = lambda n, s, dt: pctx.enter_context(nc.sbuf_tensor(n, s, dt))
        pps = lambda n, s: pctx.enter_context(nc.psum_tensor(n, s, F32))
        h2f = psb("e_h2", [128, 16 * 1024], BF16)
        gw = psb("e_gw", [128, H * NE], BF16)
        lgbf = psb("e_lg", [8, 1024], BF16)
        ltm = psb("e_ltm", [128, 64], F32)
        nmx = psb("e_nmx", [128, 1], F32)
        prb = psb("e_prb", [128, 8], F32)
        ssum = psb("e_ss", [128, 1], F32)
        rs = psb("e_rs", [128, 1], F32)
        gsv = psb("e_gs", [128, 4], F32)
        g1 = psb("e_g1", [128, 1], F32)
        gk1 = psb("e_gk1", [128, 4], F32)
        gs2 = psb("e_gs2", [128, 4], F32)
        g2 = psb("e_g2", [128, 1], F32)
        gk2 = psb("e_gk2", [128, 4], F32)
        smask = psb("e_sm", [128, 8], F32)
        mskd = psb("e_mkd", [128, 8], F32)
        m1 = psb("e_m1", [128, 1], F32)
        mk1 = psb("e_mk1", [128, 8], F32)
        ms2 = psb("e_ms2", [128, 8], F32)
        m2 = psb("e_m2", [128, 1], F32)
        mk2 = psb("e_mk2", [128, 8], F32)
        wm = psb("e_wm", [128, 64], F32)
        ybuf = psb("e_y", [128, 22 * 1024], BF16)
        wst = [psb(f"e_ws{j}", [128, 2048], BF16) for j in range(6)]
        wd = psb("e_wd", [128, 11264], BF16)
        racc = psb("e_racc", [128, 8 * D], BF16)
        sil = psb("e_sil", [128, 1024], BF16)
        xr = [psb(f"e_x{j}", [128, D], BF16) for j in range(2)]
        cen = [psb(f"e_c{j}", [128, D], F32) for j in range(2)]
        sq = [psb("e_s0", [128, D], BF16)] * 2
        st1 = [psb(f"e_st1{j}", [128, 4], F32) for j in range(2)]
        st2 = [psb(f"e_st2{j}", [128, 4], F32) for j in range(2)]
        yout = [psb("e_yo0", [128, D], F32)] * 2
        pg = pps("e_pg", [128, 1024])
        pu = pps("e_pu", [128, 1024])
        p2 = [pps(f"e_p2{j}", [128, 1024]) for j in range(2)]
        plg = p2[0][0:8, :]
        plt = pg[:, 0:8]

        dma(ph, "gpsimd", gw[:, :].rearrange("p (ic e) -> p ic e", e=NE),
            dGWT.rearrange("(ic p) e -> p ic e", p=128), waits=[mm], inc=LW)

        def h2f_loads(hf2, mm_at=None):
            trow2 = 1024 * hf2
            gend = []
            for g2 in range(4):
                for ic in range(4 * g2, 4 * g2 + 4):
                    dma(ph, "sync", h2f[:, 1024 * ic:1024 * (ic + 1)],
                        dh2[trow2:trow2 + 1024, 128 * ic:128 * (ic + 1)],
                        waits=([SX, mm] if mm_at is None else [SX]),
                        waits_at=([(mm, mm_at)] if mm_at is not None else ()),
                        inc=TQ[g2], transpose=True)
                gend.append((TQ[g2], TQ[g2].n))
            return [gend[ic // 4] for ic in range(16)]

        prev_adds_ve = None
        prev_se_end = None
        prev_slot = None
        def emit_final_ln(hf3):
            for tt in range(8):
                pp = tt % 2
                gt = 8 * hf3 + tt
                dma(ph, "sync", xr[pp][:, :], dx2[128 * gt:128 * (gt + 1), :],
                    waits=[ve], inc=LX)
                rt = racc[:, D * tt:D * (tt + 1)]
                ph.op("vector", lambda E, pp=pp, rt=rt: E.reduce_sum(st1[pp][:, 0:1], rt,
                                                                     AX.X),
                      waits=[ve], inc=ve)
                ph.op("vector", lambda E, pp=pp: E.tensor_scalar(st1[pp][:, 1:2],
                                                                 st1[pp][:, 0:1],
                                                                 1.0 / D, None, ALU.mult),
                      waits=[ve], inc=ve)
                ph.op("vector", lambda E, pp=pp, rt=rt: E.tensor_scalar(cen[pp][:, :], rt,
                                                                        st1[pp][:, 1:2],
                                                                        None, ALU.subtract),
                      waits=[ve, se], inc=ve)
                ph.op("scalar", lambda E, pp=pp: E.activation(sq[pp][:, :], cen[pp][:, :],
                                                              AF.Square,
                                                              accum_out=st2[pp][:, 0:1]),
                      waits=[ve, se], inc=se)
                ph.op("vector", lambda E, pp=pp: E.tensor_scalar(st2[pp][:, 1:2],
                                                                 st2[pp][:, 0:1],
                                                                 1.0 / D, EPS, ALU.mult,
                                                                 ALU.add),
                      waits=[se], inc=ve)
                ph.op("scalar", lambda E, pp=pp: E.activation(st2[pp][:, 2:3],
                                                              st2[pp][:, 1:2],
                                                              AF.Sqrt),
                      waits=[ve], inc=se)
                ph.op("vector", lambda E, pp=pp: E.reciprocal(st2[pp][:, 3:4],
                                                              st2[pp][:, 2:3]),
                      waits=[se], inc=ve)
                ph.op("vector", lambda E, pp=pp: E.tensor_scalar(cen[pp][:, :],
                                                                 cen[pp][:, :],
                                                                 st2[pp][:, 3:4], None,
                                                                 ALU.mult),
                      waits=[ve], inc=ve)
                ph.op("vector", lambda E, pp=pp: E.tensor_tensor(yout[pp][:, :],
                                                                 cen[pp][:, :],
                                                                 xr[pp][:, :], ALU.add),
                      waits=[ve, LX, SY], inc=ve)
                dma(ph, "sync", dY[128 * gt:128 * (gt + 1), :], yout[pp][:, :],
                    waits=[ve], inc=SY)

        lh_h2f_next = None
        for hf in range(2):
            trow = 1024 * hf
            if hf == 0:
                lh_h2f = h2f_loads(0)
            else:
                lh_h2f = lh_h2f_next
            for ic in range(16):
                for q in range(2):
                    w_at = []
                    if q == 0:
                        w_at.append(lh_h2f[ic])
                    if ic == 0 and q == 0 and hf == 1:
                        w_at.append((ve, prev_adds_ve))
                    ph.op("tensor", lambda E, ic=ic, q=q: E.matmul(
                        plg[:, 512 * q:512 * (q + 1)], gw[:, NE * ic:NE * (ic + 1)],
                        h2f[:, 1024 * ic + 512 * q:1024 * ic + 512 * (q + 1)],
                        start=(ic == 0), stop=(ic == 15)),
                        waits=(([LW, ve] if hf == 0 else [LW])
                               if (ic == 0 and q == 0) else ()),
                        waits_at=w_at,
                        inc=(mm if (ic == 15 and q == 1) else None))
            ph.op("scalar", lambda E: E.activation(lgbf[:, :], plg, AF.Copy),
                  waits=[mm], inc=se)
            lt_g = se.n
            for tt in range(8):
                ph.op("tensor", lambda E, tt=tt: E.matmul(
                    plt, lgbf[0:8, 128 * tt:128 * (tt + 1)], eye8[0:8, 0:8],
                    start=True, stop=True),
                    waits_at=[(se, lt_g)], inc=mm)
                ph.op("scalar", lambda E, tt=tt: E.activation(
                    ltm[:, 8 * tt:8 * (tt + 1)], plt, AF.Copy),
                    waits=[mm], inc=se)
                lt_g = se.n
            for tt in range(8):
                lt = ltm[:, 8 * tt:8 * (tt + 1)]
                ph.op("vector", lambda E, lt=lt: E.tensor_reduce(nmx[:, :], lt, AX.X,
                                                                 ALU.max, negate=True),
                      waits=[ve, se], inc=ve)
                ph.op("scalar", lambda E, lt=lt: E.activation(prb[:, :], lt, AF.Exp,
                                                              bias=nmx[:, 0:1],
                                                              accum_out=ssum[:, 0:1]),
                      waits=[ve, se], inc=se)
                ph.op("vector", lambda E: E.reciprocal(rs[:, :], ssum[:, :]),
                      waits=[se], inc=ve)
                ph.op("vector", lambda E: E.tensor_scalar(prb[:, :], prb[:, :],
                                                          rs[:, 0:1], None, ALU.mult),
                      waits=[ve], inc=ve)
                pe_ = prb[:, :].rearrange("p (g e) -> p g e", e=2)
                ph.op("vector", lambda E, pe_=pe_: E.tensor_tensor(
                    gsv[:, :], pe_[:, :, 0], pe_[:, :, 1], ALU.max),
                    waits=[ve], inc=ve)
                ph.op("vector", lambda E: E.tensor_reduce(g1[:, :], gsv[:, :], AX.X,
                                                          ALU.max), waits=[ve], inc=ve)
                ph.op("vector", lambda E: E.tensor_scalar(gk1[:, :], gsv[:, :],
                                                          g1[:, 0:1], None, ALU.is_ge),
                      waits=[ve], inc=ve)
                ph.op("vector", lambda E: E.tensor_scalar(gs2[:, :], gk1[:, :], -1.0, 1.0,
                                                          ALU.mult, ALU.add),
                      waits=[ve], inc=ve)
                ph.op("vector", lambda E: E.tensor_tensor(gs2[:, :], gs2[:, :], gsv[:, :],
                                                          ALU.mult), waits=[ve], inc=ve)
                ph.op("vector", lambda E: E.tensor_reduce(g2[:, :], gs2[:, :], AX.X,
                                                          ALU.max), waits=[ve], inc=ve)
                ph.op("vector", lambda E: E.tensor_scalar(gk2[:, :], gs2[:, :],
                                                          g2[:, 0:1], None, ALU.is_ge),
                      waits=[ve], inc=ve)
                ph.op("vector", lambda E: E.tensor_tensor(gk2[:, :], gk2[:, :], gk1[:, :],
                                                          ALU.max), waits=[ve], inc=ve)
                sm_ = smask[:, :].rearrange("p (g e) -> p g e", e=2)
                ph.op("vector", lambda E, sm_=sm_: E.tensor_copy(sm_[:, :, 0], gk2[:, :]),
                      waits=[ve], inc=ve)
                ph.op("vector", lambda E, sm_=sm_: E.tensor_copy(sm_[:, :, 1], gk2[:, :]),
                      waits=[ve], inc=ve)
                ph.op("vector", lambda E: E.tensor_tensor(mskd[:, :], prb[:, :],
                                                          smask[:, :], ALU.mult),
                      waits=[ve], inc=ve)
                ph.op("vector", lambda E: E.tensor_reduce(m1[:, :], mskd[:, :], AX.X,
                                                          ALU.max), waits=[ve], inc=ve)
                ph.op("vector", lambda E: E.tensor_scalar(mk1[:, :], mskd[:, :],
                                                          m1[:, 0:1], None, ALU.is_ge),
                      waits=[ve], inc=ve)
                ph.op("vector", lambda E: E.tensor_scalar(ms2[:, :], mk1[:, :], -1.0, 1.0,
                                                          ALU.mult, ALU.add),
                      waits=[ve], inc=ve)
                ph.op("vector", lambda E: E.tensor_tensor(ms2[:, :], ms2[:, :], mskd[:, :],
                                                          ALU.mult), waits=[ve], inc=ve)
                ph.op("vector", lambda E: E.tensor_reduce(m2[:, :], ms2[:, :], AX.X,
                                                          ALU.max), waits=[ve], inc=ve)
                ph.op("vector", lambda E: E.tensor_scalar(mk2[:, :], ms2[:, :],
                                                          m2[:, 0:1], None, ALU.is_ge),
                      waits=[ve], inc=ve)
                ph.op("vector", lambda E: E.tensor_tensor(mk2[:, :], mk2[:, :], mk1[:, :],
                                                          ALU.max), waits=[ve], inc=ve)
                ph.op("vector", lambda E, tt=tt: E.tensor_tensor(
                    wm[:, 8 * tt:8 * (tt + 1)], mskd[:, :], mk2[:, :], ALU.mult),
                    waits=[ve], inc=ve)
            if hf == 1:
                emit_final_ln(0)

            def ws_load(fc, ga, ua, mm_at=None):
                ws0, ws1 = wst[2 * (fc % 3)], wst[2 * (fc % 3) + 1]
                w, wa = ([mm], ()) if mm_at is None else ([], [(mm, mm_at)])
                dma(ph, "gpsimd", ws0[:, :], ga[fc], waits=w, waits_at=wa, inc=LW)
                dma(ph, "gpsimd", ws1[:, :], ua[fc], waits=w, waits_at=wa, inc=LW)

            def stage1(nfc, ga, ua, ent=None, pf_at=None, lh_ic=None):
                pgp = [pg, p2[0]]
                pup = [pu, p2[1]]
                ent_se, ent_ve = ent if ent is not None else (se.n, ve.n)
                s1se, s1ve = {}, {}
                slot_mm = {}
                lw_after = {}
                for pf in range(3):
                    if pf < nfc:
                        ws_load(pf, ga, ua,
                                None if pf_at is None else pf_at.get(pf % 3))
                        lw_after[pf] = LW.n
                for fc in range(nfc):
                    ws0, ws1 = wst[2 * (fc % 3)], wst[2 * (fc % 3) + 1]
                    pgc, puc = pgp[fc % 2], pup[fc % 2]
                    w_at = [(se, s1se.get(fc - 2, ent_se)),
                            (ve, s1ve.get(fc - 2, ent_ve)),
                            (LW, lw_after[fc])]
                    for ic in range(16):
                        for q in range(2):
                            w2 = list(w_at) if (ic == 0 and q == 0) else []
                            if fc == 0 and q == 0 and lh_ic is not None:
                                w2.append(lh_ic[ic])
                            ph.op("tensor", lambda E, ws0=ws0, ic=ic, q=q, pgc=pgc: E.matmul(
                                pgc[:, 512 * q:512 * (q + 1)],
                                ws0[:, 128 * ic:128 * (ic + 1)],
                                h2f[:, 1024 * ic + 512 * q:1024 * ic + 512 * (q + 1)],
                                start=(ic == 0), stop=(ic == 15)),
                                waits=(([LH] if lh_ic is None else [])
                                       if (ic == 0 and q == 0) else ()),
                                waits_at=w2,
                                inc=(mm if (ic == 15 and q == 1) else None))
                    for ic in range(16):
                        for q in range(2):
                            ph.op("tensor", lambda E, ws1=ws1, ic=ic, q=q, puc=puc: E.matmul(
                                puc[:, 512 * q:512 * (q + 1)],
                                ws1[:, 128 * ic:128 * (ic + 1)],
                                h2f[:, 1024 * ic + 512 * q:1024 * ic + 512 * (q + 1)],
                                start=(ic == 0), stop=(ic == 15)),
                                inc=(mm if (ic == 15 and q == 1) else None))
                    slot_mm[fc % 3] = mm.n
                    if fc + 3 < nfc:
                        ws_load(fc + 3, ga, ua)
                        lw_after[fc + 3] = LW.n
                    ph.op("scalar", lambda E, pgc=pgc: E.activation(sil[:, :], pgc[:, :],
                                                                    AF.Silu),
                          waits=[mm, ve, se], inc=se)
                    s1se[fc] = se.n
                    ph.op("vector", lambda E, fc=fc, puc=puc: E.tensor_tensor(
                        ybuf[:, 1024 * fc:1024 * (fc + 1)], puc[:, :], sil[:, :],
                        ALU.mult),
                        waits=[se], inc=ve)
                    s1ve[fc] = ve.n
                return slot_mm
            for fc in range(22):
                dma(ph, "scalar", wd[:, 512 * fc:512 * (fc + 1)],
                    dWsdT[128 * fc:128 * (fc + 1), 0:512],
                    waits=[mm], inc=LW2)
            lw2_sh = LW2.n
            last_slot = stage1(22, dWsgT, dWsuT,
                               ent=(None if hf == 0
                                    else (prev_se_end, prev_adds_ve)),
                               pf_at=(prev_slot if hf == 1 else {0: 0, 1: 0, 2: 0}),
                               lh_ic=lh_h2f)
            shc_ve = {}
            lw2A = {0: lw2_sh}
            lw2B = {0: lw2_sh}
            for oh in range(4):
                a_done = None
                for tt in range(8):
                    idx = 8 * oh + tt
                    pp = tt % 2
                    for fc in range(22):
                        w_at = []
                        if fc == 0:
                            w_at.append((LW2, lw2A[oh]))
                            if idx >= 2:
                                w_at.append((ve, shc_ve[idx - 2]))
                        if fc == 11 and tt == 0:
                            w_at.append((LW2, lw2B[oh]))
                        ph.op("tensor", lambda E, pp=pp, fc=fc, tt=tt: E.matmul(
                            p2[pp][:, 0:512],
                            ybuf[:, 1024 * fc + 128 * tt:1024 * fc + 128 * (tt + 1)],
                            wd[:, 512 * fc:512 * (fc + 1)],
                            start=(fc == 0), stop=(fc == 21)),
                            waits=([ve] if (fc == 0 and idx < 2) else ()),
                            waits_at=w_at,
                            inc=(mm if fc in (10, 21) else None))
                        if fc == 10 and tt == 7:
                            a_done = mm.n
                    ph.op("vector", lambda E, pp=pp, tt=tt, oh=oh: E.tensor_copy(
                        racc[:, D * tt + 512 * oh:D * tt + 512 * (oh + 1)],
                        p2[pp][:, 0:512]),
                        waits=[mm], inc=ve)
                    shc_ve[idx] = ve.n
                if oh < 3:
                    for fc in range(11):
                        dma(ph, "scalar", wd[:, 512 * fc:512 * (fc + 1)],
                            dWsdT[128 * fc:128 * (fc + 1),
                                  512 * (oh + 1):512 * (oh + 2)],
                            waits_at=[(mm, a_done)], inc=LW2)
                    lw2A[oh + 1] = LW2.n
                    b_at = mm.n
                    for fc in range(11, 22):
                        dma(ph, "scalar", wd[:, 512 * fc:512 * (fc + 1)],
                            dWsdT[128 * fc:128 * (fc + 1),
                                  512 * (oh + 1):512 * (oh + 2)],
                            waits_at=[(mm, b_at)], inc=LW2)
                    lw2B[oh + 1] = LW2.n
            wdb = [wd, ybuf[:, 11264:22528]]
            lw2_at = {}
            mm_after = {}

            def wd_load(k, wat):
                n2, oh2 = divmod(k, 2)
                for fc in range(11):
                    dma(ph, "scalar", wdb[k % 2][:, 1024 * fc:1024 * (fc + 1)],
                        dWdT[n2, 128 * fc:128 * (fc + 1),
                             1024 * oh2:1024 * (oh2 + 1)],
                        waits=([mm] if wat is None else []),
                        waits_at=([(mm, wat)] if wat is not None else ()),
                        inc=LW2)
                lw2_at[k] = LW2.n
            wd_load(0, None)
            wd_load(1, None)
            add_ve = {}
            for n in range(NE):
                last_slot = stage1(11, dWgT[n], dWuT[n], pf_at=last_slot,
                                   lh_ic=lh_h2f)
                if n == NE - 1 and hf == 0:
                    lh_h2f_next = h2f_loads(1, mm_at=mm.n)
                for oh in range(2):
                    k = 2 * n + oh
                    s2_se, s2_ve = se.n, ve.n
                    for tt in range(8):
                        pp = tt % 2
                        w_at = []
                        if tt == 0:
                            w_at.append((LW2, lw2_at[k]))
                        if tt >= 2:
                            w_at.append((ve, add_ve[(k, tt - 2)]))
                        elif oh == 0:
                            w_at.append((ve, s2_ve))
                            if tt == 0:
                                w_at.append((se, s2_se))
                        else:
                            w_at.append((ve, add_ve[(k - 1, 6 + tt)]))
                        for fc in range(11):
                            for q in range(2):
                                ph.op("tensor", lambda E, pp=pp, fc=fc, q=q, tt=tt,
                                      k=k: E.matmul(
                                    p2[pp][:, 512 * q:512 * (q + 1)],
                                    ybuf[:, 1024 * fc + 128 * tt:1024 * fc + 128 * (tt + 1)],
                                    wdb[k % 2][:, 1024 * fc + 512 * q:1024 * fc + 512 * (q + 1)],
                                    start=(fc == 0), stop=(fc == 10)),
                                    waits_at=(w_at if (fc == 0 and q == 0) else ()),
                                    inc=(mm if (fc == 10 and q == 1) else None))
                        for q in range(2):
                            ph.op("vector", lambda E, pp=pp, tt=tt, q=q, n=n,
                                  oh=oh: E.scalar_tensor_tensor(
                                racc[:, D * tt + 1024 * oh + 512 * q:
                                     D * tt + 1024 * oh + 512 * (q + 1)],
                                p2[pp][:, 512 * q:512 * (q + 1)],
                                wm[:, 8 * tt + n:8 * tt + n + 1],
                                racc[:, D * tt + 1024 * oh + 512 * q:
                                     D * tt + 1024 * oh + 512 * (q + 1)],
                                ALU.mult, ALU.add),
                                waits=[mm], inc=ve)
                        add_ve[(k, tt)] = ve.n
                    mm_after[k] = mm.n
                    if k + 2 < 2 * NE:
                        wd_load(k + 2, mm_after[k])
            prev_adds_ve = add_ve[(2 * NE - 1, 7)]
            prev_se_end = se.n
            prev_slot = last_slot
            if hf == 1:
                emit_final_ln(1)

        ph.op("sync", lambda E: E.nop(), waits=[SY])
        ph.play()

    ctx.close()
    return nc


def _mk_mask(r):
    mk = np.zeros((128, 4 * 128), np.float32)
    for kk in range(4):
        if kk < r:
            mk[:, 128 * kk:128 * (kk + 1)] = 1.0
        elif kk == r:
            mk[:, 128 * kk:128 * (kk + 1)] = np.triu(np.ones((128, 128), np.float32))
    return mk


def kernel(L0_attn_X, L0_attn_Wq, L0_attn_bq, L0_attn_Wk, L0_attn_bk,
           L0_attn_Wv, L0_attn_bv, L0_attn_Wo, L0_attn_bo,
           L0_ffn_gate_weight, L0_ffn_gate_w, L0_ffn_up_w, L0_ffn_down_w,
           L0_ffn_sh_gate_w, L0_ffn_sh_up_w, L0_ffn_sh_down_w):
    X = np.ascontiguousarray(np.asarray(L0_attn_X, dtype=np.float32))
    tr = lambda a: np.ascontiguousarray(np.asarray(a, np.float32).T)
    row = lambda a: np.ascontiguousarray(np.asarray(a, np.float32).reshape(1, -1))

    def swz(a):
        F_, D_ = a.shape
        return np.ascontiguousarray(
            a.reshape(F_ // 128, 128, D_ // 128, 128).transpose(0, 3, 2, 1)
            .reshape(F_ // 128, 128, D_))
    BF = ml_dtypes.bfloat16
    bf = lambda a: np.ascontiguousarray(a.astype(BF))
    WqT, WkT, WvT, WoT = (bf(tr(L0_attn_Wq)), bf(tr(L0_attn_Wk)),
                          bf(tr(L0_attn_Wv)), bf(tr(L0_attn_Wo)))
    GWT = bf(tr(L0_ffn_gate_weight))
    gw_ = np.asarray(L0_ffn_gate_w, np.float32)
    up_ = np.asarray(L0_ffn_up_w, np.float32)
    WgT = bf(np.stack([swz(gw_[n]) for n in range(NE)]))
    WuT = bf(np.stack([swz(up_[n]) for n in range(NE)]))
    WdT = bf(np.asarray(L0_ffn_down_w, np.float32).transpose(0, 2, 1))
    WsgT = bf(swz(np.asarray(L0_ffn_sh_gate_w, np.float32)))
    WsuT = bf(swz(np.asarray(L0_ffn_sh_up_w, np.float32)))
    WsdT = bf(tr(L0_ffn_sh_down_w))

    nc = build_program()
    in_maps = []
    for c in range(8):
        b, r = c // 4, c % 4
        Xb = X[b]
        tiles = Xb.reshape(64, 128, D)
        Xq = np.ascontiguousarray(tiles[[4 * j + r for j in range(16)]]).reshape(TOW, D)
        in_maps.append({
            "Xq": Xq, "Xkv": Xb,
            "WqT": WqT, "WkT": WkT, "WvT": WvT, "WoT": WoT,
            "bq": row(L0_attn_bq), "bk": row(L0_attn_bk),
            "bv": row(L0_attn_bv), "bo": row(L0_attn_bo),
            "GWT": GWT, "WgT": WgT, "WuT": WuT, "WdT": WdT,
            "WsgT": WsgT, "WsuT": WsuT, "WsdT": WsdT,
            "MK01": _mk_mask(r),
        })
    res = run_bass_kernel_spmd(nc, in_maps, core_ids=list(range(8)))
    Y = np.empty((2, SBT, D), np.float32)
    for c in range(8):
        b, r = c // 4, c % 4
        yo = np.asarray(res.results[c]["Yo"]).reshape(16, 128, D)
        for j in range(16):
            p = 4 * j + r
            Y[b, 128 * p:128 * (p + 1), :] = yo[j]
    return Y


# version 23
# speedup vs baseline: 1.0322x; 1.0322x over previous
import math
import numpy as np
import ml_dtypes
import concourse.bass as bass
import concourse.mybir as mybir
from concourse.bass_utils import run_bass_kernel_spmd
from contextlib import ExitStack

F32 = mybir.dt.float32
BF16 = mybir.dt.bfloat16
AF = mybir.ActivationFunctionType
ALU = mybir.AluOpType
AX = mybir.AxisListType

D = 2048
SBT = 8192
H = 16
TOW = 2048
NE = 8
FF = 1408
FS = 2816
S4 = float(1.0 / math.sqrt(math.sqrt(128.0)))
EPS = 1e-5
NEGBIG = -1.0e30


class Sem:
    def __init__(self, h):
        self.h = h
        self.n = 0


class Phase:

    ENGS = ("sync", "tensor", "vector", "scalar", "gpsimd")

    def __init__(self, nc, name):
        self.nc = nc
        self.name = name
        self.q = {e: [] for e in self.ENGS}

    def op(self, eng, fn, waits=(), inc=None, inc_amt=1, waits_at=()):
        wv = [(s.h, s.n) for s in waits if s.n > 0]
        wv += [(s.h, v) for (s, v) in waits_at if v > 0]
        if inc is not None:
            inc.n += inc_amt

        def run(E, fn=fn, wv=wv, inc=inc, inc_amt=inc_amt):
            for h, v in wv:
                E.wait_ge(h, v)
            inst = fn(E)
            if inc is not None:
                inst.then_inc(inc.h, inc_amt)
        self.q[eng].append(run)

    def play(self):
        with self.nc.Block(self.name) as b:
            for e in self.ENGS:
                ops = self.q[e]
                if not ops:
                    continue
                getattr(b, e)(lambda E, ops=ops: [f(E) for f in ops])


def dma(ph, eng, out, in_, waits=(), inc=None, transpose=False, waits_at=()):
    if transpose:
        ph.op(eng, lambda E: E.dma_start(out=out, in_=in_, transpose=True),
              waits=waits, inc=inc, inc_amt=16, waits_at=waits_at)
    else:
        ph.op(eng, lambda E: E.dma_start(out=out, in_=in_),
              waits=waits, inc=inc, inc_amt=16, waits_at=waits_at)


def build_program(taps=()):
    nc = bass.Bass("TRN2", target_bir_lowering=False)
    tapset = set(taps)

    def dram(name, shape, dt):
        k = "ExternalOutput" if name in tapset else "Internal"
        return nc.dram_tensor(name, shape, dt, kind=k).ap()

    dXq = nc.dram_tensor("Xq", [TOW, D], F32, kind="ExternalInput").ap()
    dXkv = nc.dram_tensor("Xkv", [SBT, D], F32, kind="ExternalInput").ap()
    dWqT = nc.dram_tensor("WqT", [D, D], BF16, kind="ExternalInput").ap()
    dWkT = nc.dram_tensor("WkT", [D, D], BF16, kind="ExternalInput").ap()
    dWvT = nc.dram_tensor("WvT", [D, D], BF16, kind="ExternalInput").ap()
    dWoT = nc.dram_tensor("WoT", [D, D], BF16, kind="ExternalInput").ap()
    dbq = nc.dram_tensor("bq", [1, D], F32, kind="ExternalInput").ap()
    dbk = nc.dram_tensor("bk", [1, D], F32, kind="ExternalInput").ap()
    dbv = nc.dram_tensor("bv", [1, D], F32, kind="ExternalInput").ap()
    dbo = nc.dram_tensor("bo", [1, D], F32, kind="ExternalInput").ap()
    dGWT = nc.dram_tensor("GWT", [D, NE], BF16, kind="ExternalInput").ap()
    dWgT = nc.dram_tensor("WgT", [NE, FF // 128, 128, D], BF16, kind="ExternalInput").ap()
    dWuT = nc.dram_tensor("WuT", [NE, FF // 128, 128, D], BF16, kind="ExternalInput").ap()
    dWdT = nc.dram_tensor("WdT", [NE, FF, D], BF16, kind="ExternalInput").ap()
    dWsgT = nc.dram_tensor("WsgT", [FS // 128, 128, D], BF16, kind="ExternalInput").ap()
    dWsuT = nc.dram_tensor("WsuT", [FS // 128, 128, D], BF16, kind="ExternalInput").ap()
    dWsdT = nc.dram_tensor("WsdT", [FS, D], BF16, kind="ExternalInput").ap()
    dMK = nc.dram_tensor("MK01", [128, 4 * 128], F32, kind="ExternalInput").ap()

    dY = nc.dram_tensor("Yo", [TOW, D], F32, kind="ExternalOutput").ap()

    dh1kv = dram("h1kv", [SBT, D], BF16)
    dh1q = dram("h1q", [TOW, D], BF16)
    dKfm = dram("Kfm", [D, SBT], BF16)
    dVtm = dram("Vtm", [SBT, D], BF16)
    dQfm = dram("Qfm", [D, TOW], BF16)
    dx2 = dram("x2sp", [TOW, D], BF16)
    dh2 = dram("h2rt", [TOW, D], BF16)
    dAtt = dram("attsp", [128, H * TOW], BF16) if "attsp" in tapset else None

    ctx = ExitStack()
    sbuf = lambda n, s, dt: ctx.enter_context(nc.sbuf_tensor(n, s, dt))
    sem = lambda n: Sem(ctx.enter_context(nc.semaphore(n)))

    mm = sem("mm")
    ve = sem("ve")
    se = sem("se")
    ge = sem("ge")
    LX = sem("LX")
    LH = sem("LH")
    LW = sem("LW")
    LK = sem("LK")
    LW2 = sem("LW2")
    TQ = [sem(f"TQ{j}") for j in range(4)]
    SH = [sem(f"SHa{i}") for i in range(4)]
    SH2 = sem("SH2")
    SKV = sem("SKV")
    SX = sem("SX")
    SY = sem("SY")
    all_stores = SH + [SH2, SKV, SX, SY]
    all_sems = [mm, ve, se, ge, LX, LH, LW, LK, LW2] + TQ + all_stores

    def fence(ph):
        for eng in Phase.ENGS:
            ph.op(eng, lambda E: E.nop(), waits=list(all_sems))

    ones_col = sbuf("ones_col", [128, 1], BF16)
    ones_row = sbuf("ones_row", [1, 128], BF16)
    ones_rowf = sbuf("ones_rowf", [1, 128], F32)
    negbig_row = sbuf("negbig_row", [1, 512], BF16)
    onef = sbuf("onef", [1, 1], F32)
    mk01 = sbuf("mk01", [128, 4 * 128], BF16)
    eye8 = sbuf("eye8", [8, 8], BF16)
    eye8f = sbuf("eye8f", [8, 8], F32)
    bqc = sbuf("bqc", [128, H], F32)
    bkc = sbuf("bkc", [128, H], F32)
    bv_bf = sbuf("bv_bf", [1, D], BF16)
    bo_bf = sbuf("bo_bf", [1, D], BF16)

    ph = Phase(nc, "p0")
    with ExitStack() as pctx:
        pc0 = pctx.enter_context(nc.psum_tensor("pc0", [128, 1], F32))
        brow = pctx.enter_context(nc.sbuf_tensor("brow", [1, D], F32))
        brow2 = pctx.enter_context(nc.sbuf_tensor("brow2", [1, D], F32))
        dma(ph, "sync", brow[:, :], dbq, inc=LX)
        dma(ph, "sync", brow2[:, :], dbk, inc=LX)
        dma(ph, "gpsimd", bv_bf[:, :], dbv, inc=LW)
        dma(ph, "gpsimd", bo_bf[:, :], dbo, inc=LW)
        dma(ph, "gpsimd", mk01[:, :], dMK, inc=LW)
        ph.op("gpsimd", lambda E: E.iota(eye8f[:, :], pattern=[[1, 8]], base=0,
                                         channel_multiplier=-1,
                                         allow_small_or_imprecise_dtypes=True), inc=ge)
        ph.op("vector", lambda E: E.memset(ones_col[:, :], 1.0), inc=ve)
        ph.op("vector", lambda E: E.memset(ones_row[:, :], 1.0), inc=ve)
        ph.op("vector", lambda E: E.memset(ones_rowf[:, :], 1.0), inc=ve)
        ph.op("vector", lambda E: E.memset(negbig_row[:, :], NEGBIG), inc=ve)
        ph.op("vector", lambda E: E.memset(onef[:, :], 1.0), inc=ve)
        ph.op("vector", lambda E: E.tensor_scalar(eye8f[:, :], eye8f[:, :], 0.0, None,
                                                  ALU.is_equal), waits=[ge], inc=ve)
        ph.op("vector", lambda E: E.tensor_copy(eye8[:, :], eye8f[:, :]),
              waits=[ve], inc=ve)
        ph.op("vector", lambda E: E.tensor_scalar(brow[:, :], brow[:, :], S4, None,
                                                  ALU.mult), waits=[LX], inc=ve)
        ph.op("vector", lambda E: E.tensor_scalar(brow2[:, :], brow2[:, :], S4, None,
                                                  ALU.mult), waits=[LX], inc=ve)
        for h in range(H):
            ph.op("tensor", lambda E, h=h: E.matmul(pc0[:, :],
                                                    brow[0:1, 128 * h:128 * (h + 1)],
                                                    onef[0:1, 0:1], start=True, stop=True),
                  waits=[ve], inc=mm)
            ph.op("vector", lambda E, h=h: E.tensor_copy(bqc[:, h:h + 1], pc0[:, :]),
                  waits=[mm], inc=ve)
            ph.op("tensor", lambda E, h=h: E.matmul(pc0[:, :],
                                                    brow2[0:1, 128 * h:128 * (h + 1)],
                                                    onef[0:1, 0:1], start=True, stop=True),
                  waits=[ve], inc=mm)
            ph.op("vector", lambda E, h=h: E.tensor_copy(bkc[:, h:h + 1], pc0[:, :]),
                  waits=[mm], inc=ve)
        ph.play()

    def load_w_strips(ph, wdram, dst, waits=(), waits_at=()):
        lw_strips = []
        for ic in range(16):
            dma(ph, "gpsimd", dst[:, D * ic:D * (ic + 1)],
                wdram[128 * ic:128 * (ic + 1), :], waits=waits,
                waits_at=waits_at, inc=LW)
            lw_strips.append(LW.n)
        return lw_strips

    ph = Phase(nc, "pab")
    fence(ph)
    with ExitStack() as pctx:
        psb = lambda n, s, dt: pctx.enter_context(nc.sbuf_tensor(n, s, dt))
        pps = lambda n, s: pctx.enter_context(nc.psum_tensor(n, s, F32))
        xin = [psb(f"pa_x{j}", [128, D], F32) for j in range(2)]
        cen = [psb(f"pa_c{j}", [128, D], F32) for j in range(2)]
        sq = [psb("pa_s0", [128, D], BF16)] * 2
        st1 = [psb(f"pa_m{j}", [128, 2], F32) for j in range(2)]
        st2 = [psb(f"pa_v{j}", [128, 4], F32) for j in range(2)]
        h1o = [psb(f"pa_o{j}", [128, D], BF16) for j in range(2)]
        wbig = psb("b_w", [128, 16 * D], BF16)
        h1f = psb("b_h", [128, 16 * D], BF16)
        est = [psb(f"b_e{j}", [128, D], BF16) for j in range(2)]
        pk = [pps(f"b_p{j}", [128, 2048]) for j in range(2)]

        ln_ve = {}
        ln_store_g = {}
        ln_sq_se = {}

        def emit_ln_tile(i):
            if i < 64:
                src, dst, ssem = dXkv[128 * i:128 * (i + 1), :], \
                    dh1kv[128 * i:128 * (i + 1), :], SH[i // 16]
            else:
                j = i - 64
                src, dst, ssem = dXq[128 * j:128 * (j + 1), :], \
                    dh1q[128 * j:128 * (j + 1), :], SH2
            sl = i % 2
            dma(ph, "scalar", xin[sl][:, :], src,
                waits=([ve] if i < 2 else []),
                waits_at=([(ve, ln_ve[i - 2])] if i >= 2 else ()), inc=LX)
            ph.op("vector", lambda E, sl=sl: E.reduce_sum(st1[sl][:, 0:1],
                                                          xin[sl][:, :], AX.X),
                  waits=[LX], inc=ve)
            ph.op("vector", lambda E, sl=sl: E.tensor_scalar(st1[sl][:, 1:2],
                                                             st1[sl][:, 0:1],
                                                             1.0 / D, None, ALU.mult),
                  waits=[ve], inc=ve)
            ph.op("vector", lambda E, sl=sl: E.tensor_scalar(cen[sl][:, :], xin[sl][:, :],
                                                             st1[sl][:, 1:2], None,
                                                             ALU.subtract),
                  waits=[ve],
                  waits_at=([(se, ln_sq_se[i - 2])] if i >= 2 else ()), inc=ve)
            ph.op("scalar", lambda E, sl=sl: E.activation(sq[sl][:, :], cen[sl][:, :],
                                                          AF.Square,
                                                          accum_out=st2[sl][:, 0:1]),
                  waits=[ve], inc=se)
            ln_sq_se[i] = se.n
            ph.op("vector", lambda E, sl=sl: E.tensor_scalar(st2[sl][:, 1:2],
                                                             st2[sl][:, 0:1],
                                                             1.0 / D, EPS, ALU.mult,
                                                             ALU.add),
                  waits=[se], inc=ve)
            ph.op("scalar", lambda E, sl=sl: E.activation(st2[sl][:, 2:3], st2[sl][:, 1:2],
                                                          AF.Sqrt), waits=[ve], inc=se)
            ph.op("vector", lambda E, sl=sl: E.reciprocal(st2[sl][:, 3:4], st2[sl][:, 2:3]),
                  waits=[se], inc=ve)
            ph.op("vector", lambda E, sl=sl: E.tensor_scalar(h1o[sl][:, :], cen[sl][:, :],
                                                             st2[sl][:, 3:4], None,
                                                             ALU.mult),
                  waits=([ve] if i < 2 else [ve]),
                  waits_at=([ln_store_g[i - 2]] if i >= 2 else ()), inc=ve)
            dma(ph, "gpsimd", dst, h1o[sl][:, :], waits=[ve], inc=ssem)
            ln_store_g[i] = (ssem, ssem.n)
            ln_ve[i] = ve.n

        lw_s = load_w_strips(ph, dWkT, wbig, waits=[mm])
        for i in range(16):
            emit_ln_tile(i)
        ln_next = 16

        se_ev = {}
        skv_g = {}
        uu = 0

        def evac_store(pp, fn, dst):
            nonlocal uu
            ph.op("scalar", fn,
                  waits=[mm],
                  waits_at=([(SKV, skv_g[uu - 2])] if uu >= 2 else ()), inc=se)
            se_ev[uu] = se.n
            dma(ph, "gpsimd", dst, est[pp][:, :], waits=[se], inc=SKV)
            skv_g[uu] = SKV.n
            uu += 1

        def quarter_transposes(src, tile0, ssem, tb, sec_end, qsec=None):
            lh_qq = []
            for qq in range(4):
                gv = ln_store_g[tile0 + 4 * qq + 3]
                se2 = qsec[qq] if qsec is not None else sec_end
                for ic in range(16):
                    dma(ph, "sync",
                        h1f[:, D * ic + 512 * qq:D * ic + 512 * (qq + 1)],
                        src[512 * qq:512 * (qq + 1), 128 * ic:128 * (ic + 1)],
                        waits_at=[gv] + ([(mm, se2)]
                                         if se2 is not None else []),
                        inc=TQ[qq], transpose=True)
                lh_qq.append((TQ[qq], TQ[qq].n))
            return lh_qq

        sec_end = None
        kq_prev = None
        for tb in range(4):
            lh_qq = quarter_transposes(dh1kv[2048 * tb:2048 * (tb + 1), :],
                                       16 * tb, SH[tb], tb, sec_end,
                                       qsec=kq_prev)
            kq_cur = [None] * 4
            for h in range(H):
                pp = h % 2
                w_at0 = [(LW, lw_s[15])]
                if uu >= 2:
                    w_at0.append((se, se_ev[uu - 2]))
                for ic in range(16):
                    for q in range(4):
                        w_at = []
                        if ic == 0 and q == 0:
                            w_at += w_at0
                        if ic == 0 and h == 0:
                            w_at.append(lh_qq[q])
                        ph.op("tensor", lambda E, pp=pp, ic=ic, q=q, h=h: E.matmul(
                            pk[pp][:, 512 * q:512 * (q + 1)],
                            wbig[:, D * ic + 128 * h:D * ic + 128 * (h + 1)],
                            h1f[:, D * ic + 512 * q:D * ic + 512 * (q + 1)],
                            start=(ic == 0), stop=(ic == 15)),
                            waits_at=w_at,
                            inc=(mm if (ic == 15 and (q == 3 or h == H - 1))
                                 else None))
                        if ic == 15 and h == H - 1:
                            kq_cur[q] = mm.n
                evac_store(pp, lambda E, pp=pp, h=h: E.activation(
                    est[pp][:, :], pk[pp][:, :], AF.Identity,
                    bias=bkc[:, h:h + 1], scale=S4),
                    dKfm[128 * h:128 * (h + 1), 2048 * tb:2048 * (tb + 1)])
                if ln_next < 80:
                    emit_ln_tile(ln_next)
                    ln_next += 1
            kq_prev = kq_cur
            sec_end = mm.n

        lw_s = load_w_strips(ph, dWvT, wbig, waits_at=[(mm, sec_end)])
        vq_prev = None
        for tb in range(4):
            lh_qq = quarter_transposes(dh1kv[2048 * tb:2048 * (tb + 1), :],
                                       16 * tb, SH[tb], tb, sec_end,
                                       qsec=vq_prev)
            vq_cur = [None] * 4
            for tt in range(16):
                pp = tt % 2
                w_at0 = []
                if uu >= 2:
                    w_at0.append((se, se_ev[uu - 2]))
                for ic in range(16):
                    for q in range(4):
                        w_at = []
                        if ic == 0 and q == 0:
                            w_at += w_at0
                            w_at.append(lh_qq[tt // 4])
                        if q == 0 and tb == 0 and tt == 0:
                            w_at.append((LW, lw_s[ic]))
                        ph.op("tensor", lambda E, pp=pp, ic=ic, q=q, tt=tt: E.matmul(
                            pk[pp][:, 512 * q:512 * (q + 1)],
                            h1f[:, D * ic + 128 * tt:D * ic + 128 * (tt + 1)],
                            wbig[:, D * ic + 512 * q:D * ic + 512 * (q + 1)],
                            start=(ic == 0), stop=False),
                            waits_at=w_at)
                for q in range(4):
                    ph.op("tensor", lambda E, pp=pp, q=q: E.matmul(
                        pk[pp][:, 512 * q:512 * (q + 1)],
                        ones_row[0:1, :], bv_bf[0:1, 512 * q:512 * (q + 1)],
                        start=False, stop=True),
                        inc=(mm if q == 3 else None))
                if tt % 4 == 3:
                    vq_cur[tt // 4] = mm.n
                evac_store(pp, lambda E, pp=pp: E.activation(
                    est[pp][:, :], pk[pp][:, :], AF.Copy),
                    dVtm[2048 * tb + 128 * tt:2048 * tb + 128 * (tt + 1), :])
                if ln_next < 80:
                    emit_ln_tile(ln_next)
                    ln_next += 1
            vq_prev = vq_cur
            sec_end = mm.n

        lw_s = load_w_strips(ph, dWqT, wbig, waits_at=[(mm, sec_end)])
        lh_qq = quarter_transposes(dh1q, 64, SH2, 0, sec_end)
        for h in range(H):
            pp = h % 2
            w_at0 = []
            if uu >= 2:
                w_at0.append((se, se_ev[uu - 2]))
            for ic in range(16):
                for q in range(4):
                    w_at = []
                    if ic == 0 and q == 0:
                        w_at += w_at0
                    if ic == 0 and h == 0:
                        w_at.append(lh_qq[q])
                    if q == 0 and h == 0:
                        w_at.append((LW, lw_s[ic]))
                    ph.op("tensor", lambda E, pp=pp, ic=ic, q=q, h=h: E.matmul(
                        pk[pp][:, 512 * q:512 * (q + 1)],
                        wbig[:, D * ic + 128 * h:D * ic + 128 * (h + 1)],
                        h1f[:, D * ic + 512 * q:D * ic + 512 * (q + 1)],
                        start=(ic == 0), stop=(ic == 15)),
                        waits_at=w_at,
                        inc=(mm if (ic == 15 and q == 3) else None))
            evac_store(pp, lambda E, pp=pp, h=h: E.activation(
                est[pp][:, :], pk[pp][:, :], AF.Identity,
                bias=bqc[:, h:h + 1], scale=S4),
                dQfm[128 * h:128 * (h + 1), :])
        ph.play()

    att_ctx = ExitStack()
    att = att_ctx.enter_context(nc.sbuf_tensor("att", [128, H * TOW], BF16))

    ph = Phase(nc, "pc")
    fence(ph)
    with ExitStack() as pctx:
        psb = lambda n, s, dt: pctx.enter_context(nc.sbuf_tensor(n, s, dt))
        pps = lambda n, s: pctx.enter_context(nc.psum_tensor(n, s, F32))
        kf = [psb(f"c_k{j}", [128, SBT], BF16) for j in range(2)]
        vf = [psb(f"c_v{j}", [128, SBT], BF16) for j in range(2)]
        qf = [psb(f"c_q{j}", [128, TOW], BF16) for j in range(2)]
        sqs = [psb(f"c_sqs{j}", [128, SBT + TOW], BF16) for j in range(2)]
        mxq = psb("c_mq", [1, 2], F32)
        mxk = psb("c_mk", [1, 2], F32)
        bsq = psb("c_bsq", [1, 2], F32)
        bneg = psb("c_bn", [1, 2], F32)
        bcol = [psb(f"c_bc{j}", [128, 1], F32) for j in range(2)]
        PT = [psb(f"c_pt{j}", [128, 1024], BF16) for j in range(4)]
        lrow = psb("c_lr", [1, 512], F32)
        rl = psb("c_rl", [1, 512], F32)
        bcsb = psb("c_bcs", [128, 512], F32)
        pST = [pps(f"c_ps{j}", [128, 1024]) for j in range(2)]
        pAV2 = [pps(f"c_pav{j}", [128, 512]) for j in range(2)]
        pl = pps("c_pl", [1, 512])
        pbc = pps("c_pbc", [128, 512])

        accq = psb("c_aq", [1, 512], F32)
        acck = psb("c_ak", [1, 512], F32)
        def emit_bounds_sq(hh):
            hhs = hh % 2
            sq2 = sqs[hhs]
            ph.op("vector", lambda E, hhs=hhs, sq2=sq2: E.tensor_tensor(
                sq2[:, SBT:SBT + TOW], qf[hhs][:, :], qf[hhs][:, :], ALU.mult),
                waits_at=[(LK, lk_after[hh]), (mm, bounds_mm.get(hh - 2, 0))], inc=ve)
            ph.op("vector", lambda E, hhs=hhs, sq2=sq2: E.tensor_tensor(
                sq2[:, 0:SBT], kf[hhs][:, :], kf[hhs][:, :], ALU.mult),
                waits_at=[(LK, lk_after[hh]), (mm, bounds_mm.get(hh - 2, 0))], inc=ve)

        def emit_bounds(hh):
            hhs = hh % 2
            sq2 = sqs[hhs]
            nbank = [pbc[0:1, :], pl[0:1, 0:512]]

            def normmax(acc, nch, base=SBT):
                for j in range(nch):
                    bk = nbank[j % 2]
                    ph.op("tensor", lambda E, j=j, bk=bk, sq2=sq2, base=base: E.matmul(
                        bk, ones_col[:, 0:1],
                        sq2[:, base + 512 * j:base + 512 * (j + 1)],
                        start=True, stop=True),
                        waits=[ve], inc=mm)
                    if j == 0:
                        ph.op("vector", lambda E, bk=bk, acc=acc: E.tensor_copy(
                            acc[:, :], bk), waits=[mm], inc=ve)
                    else:
                        ph.op("vector", lambda E, bk=bk, acc=acc: E.tensor_tensor(
                            acc[:, :], acc[:, :], bk, ALU.max),
                            waits=[mm], inc=ve)

            def normmax2(acc, nch):
                normmax(acc, nch, base=0)
            normmax(accq, TOW // 512)
            normmax2(acck, SBT // 512)
            ph.op("vector", lambda E, hhs=hhs: E.tensor_reduce(
                mxq[0:1, hhs:hhs + 1], accq[:, :], AX.X, ALU.max),
                waits=[ve], inc=ve)
            ph.op("vector", lambda E, hhs=hhs: E.tensor_reduce(
                mxk[0:1, hhs:hhs + 1], acck[:, :], AX.X, ALU.max),
                waits=[ve], inc=ve)
            ph.op("vector", lambda E, hhs=hhs: E.tensor_tensor(
                bsq[0:1, hhs:hhs + 1], mxq[0:1, hhs:hhs + 1], mxk[0:1, hhs:hhs + 1],
                ALU.mult), waits=[ve], inc=ve)
            ph.op("scalar", lambda E, hhs=hhs: E.activation(
                bsq[0:1, hhs:hhs + 1], bsq[0:1, hhs:hhs + 1], AF.Sqrt),
                waits=[ve], inc=se)
            ph.op("vector", lambda E, hhs=hhs: E.tensor_scalar(
                bneg[0:1, hhs:hhs + 1], bsq[0:1, hhs:hhs + 1], -1.0, None, ALU.mult),
                waits=[se], inc=ve)
            ph.op("tensor", lambda E, hhs=hhs: E.matmul(
                pbc[:, 0:1], ones_rowf[0:1, :], bneg[0:1, hhs:hhs + 1],
                start=True, stop=True),
                waits=[ve], inc=mm)
            ph.op("vector", lambda E, hhs=hhs: E.tensor_copy(
                bcol[hhs][:, :], pbc[:, 0:1]),
                waits=[mm, se], inc=ve)

        hd_end = {}
        lk_after = {}
        bounds_mm = {}
        attmul_ve = {}
        lrowcp_ve = {}

        def head_loads(h):
            hs = h % 2
            w_at = [(mm, hd_end.get(h - 2, (0, 0))[0]),
                    (se, hd_end.get(h - 2, (0, 0))[1])]
            dma(ph, "sync", kf[hs][:, :], dKfm[128 * h:128 * (h + 1), :],
                waits=[SKV], waits_at=w_at, inc=LK)
            dma(ph, "sync", qf[hs][:, :], dQfm[128 * h:128 * (h + 1), :],
                inc=LK)
            dma(ph, "sync",
                vf[hs][:, :].rearrange("p (a d) -> p a d", d=128),
                dVtm[:, 128 * h:128 * (h + 1)].rearrange("(a p) d -> p a d", p=128),
                inc=LK)
            lk_after[h] = LK.n
        head_loads(0)
        head_loads(1)
        for h in range(H):
            hs = h % 2
            hh_cur = h
            head_start_se = se.n
            if h == 0:
                emit_bounds_sq(0)
                emit_bounds(0)
                bounds_mm[0] = mm.n

            bcol_ve = ve.n
            seq = []
            gidx = 0

            def emit_ST(t, rec):
                sl = t % 2
                c, u, m = rec["c"], rec["u"], rec["m"]
                if t >= 2:
                    w_at = [(se, seq[t - 2]["se_post"])]
                else:
                    w_at = [(se, head_start_se), (ve, bcol_ve)]
                if rec["hc"] == 0:
                    w_at = w_at + [(LK, lk_after[hh_cur])]
                w_first = []
                for kki in range(2):
                    kk = 2 * rec["half"] + kki
                    seg = 512 * kki
                    if u > 0:
                        ph.op("tensor", lambda E, sl=sl, seg=seg, u=u: E.matmul(
                            pST[sl][:, seg:seg + 128 * u],
                            ones_row[0:1, :], negbig_row[0:1, 0:128 * u],
                            start=True, stop=True),
                            waits=(w_first if kki == 0 else ()),
                            waits_at=(w_at if kki == 0 else ()))
                        w_st, w_sat = (), ()
                    else:
                        w_st = w_first if kki == 0 else ()
                        w_sat = w_at if kki == 0 else ()
                    ph.op("tensor", lambda E, sl=sl, seg=seg, u=u, c=c, kk=kk,
                          m=m, hs=hs: E.matmul(
                        pST[sl][:, seg + 128 * u:seg + 512],
                        kf[hs][:, 128 * (4 * c + kk):128 * (4 * c + kk + 1)],
                        qf[hs][:, 512 * m + 128 * u:512 * (m + 1)],
                        start=True, stop=True),
                        waits=w_st, waits_at=w_sat,
                        inc=(mm if kki == 1 else None))
                rec["mm_st"] = mm.n

            def emit_exp(t, rec):
                sl = t % 2
                pt = PT[t % 4]
                w_at = [(mm, rec["mm_st"]), (ve, bcol_ve)]
                ph.op("scalar", lambda E, sl=sl, pt=pt, bc=bcol[hh_cur % 2]: E.activation(
                    pt[:, :], pST[sl][:, :], AF.Exp, bias=bc[:, 0:1]),
                    waits_at=w_at, inc=se)
                rec["se_post"] = se.n
                if rec["diag"]:
                    for kki in range(2):
                        kk = 2 * rec["half"] + kki
                        seg = 512 * kki
                        ph.op("vector", lambda E, pt=pt, seg=seg, ud=rec["ud"],
                              kk=kk: E.tensor_tensor(
                            pt[:, seg + 128 * ud:seg + 128 * (ud + 1)],
                            pt[:, seg + 128 * ud:seg + 128 * (ud + 1)],
                            mk01[:, 128 * kk:128 * (kk + 1)], ALU.mult),
                            waits=[LW], waits_at=[(se, rec["se_post"])], inc=ve)
                rec["ve_post"] = ve.n

            def emit_AV(t, rec):
                pt = PT[t % 4]
                first, last = rec["first"], rec["last"]
                g = 4 * hh_cur + rec["m"]
                pav = pAV2[g % 2]
                w_at = [(se, rec["se_post"])]
                if rec["diag"]:
                    w_at.append((ve, rec["ve_post"]))
                for kki in range(2):
                    kk = 2 * rec["half"] + kki
                    seg = 512 * kki
                    av_first = (first and kki == 0)
                    av_last = (last and kki == 1)
                    ph.op("tensor", lambda E, pt=pt, seg=seg, c=rec["c"], kk=kk,
                          hs=hs, af=av_first, al=av_last, pav=pav: E.matmul(
                        pav[:, :],
                        vf[hs][:, 128 * (4 * c + kk):128 * (4 * c + kk + 1)],
                        pt[:, seg:seg + 512],
                        start=af, stop=al),
                        waits_at=(w_at + [(ve, attmul_ve.get(g - 2, 0))]
                                  if av_first else (w_at if kki == 0 else ())),
                        inc=(mm if kki == 1 else None))
                rec["av_mm"] = mm.n

            def emit_L(t, rec):
                pt = PT[t % 4]
                first, last = rec["first"], rec["last"]
                g = 4 * hh_cur + rec["m"]
                w_at = [(se, rec["se_post"])]
                if rec["diag"]:
                    w_at.append((ve, rec["ve_post"]))
                if first:
                    w_at.append((ve, lrowcp_ve.get(g - 1, 0)))
                ph.op("tensor", lambda E, pt=pt, first=first: E.matmul(
                    pl[:, 0:512], ones_col[:, 0:1], pt[:, 0:512],
                    start=first, stop=False),
                    waits_at=w_at)
                ph.op("tensor", lambda E, pt=pt, last=last: E.matmul(
                    pl[:, 0:512], ones_col[:, 0:1], pt[:, 512:1024],
                    start=False, stop=last),
                    inc=mm)
                rec["l_mm"] = mm.n

            def emit_finalize_a(m, tE):
                g = 4 * h + m
                ph.op("vector", lambda E: E.tensor_copy(lrow[0:1, 0:512], pl[:, :]),
                      waits_at=[(mm, seq[tE]["l_mm"])], inc=ve)
                lrowcp_ve[g] = ve.n
                ph.op("vector", lambda E: E.reciprocal(rl[:, :], lrow[0:1, 0:512]),
                      waits=[ve], inc=ve)
                return ve.n

            def emit_finalize_b(m, rec_ve):
                g = 4 * h + m
                ph.op("tensor", lambda E: E.matmul(pbc[:, :], ones_rowf[0:1, :],
                                                   rl[0:1, :], start=True, stop=True),
                      waits_at=[(ve, rec_ve)], inc=mm)
                ph.op("vector", lambda E: E.tensor_copy(bcsb[:, :], pbc[:, :]),
                      waits=[mm], inc=ve)
                ph.op("vector", lambda E, h=h, m=m, g=g: E.tensor_tensor(
                    att[:, TOW * h + 512 * m:TOW * h + 512 * (m + 1)],
                    pAV2[g % 2][:, :], bcsb[:, :], ALU.mult),
                    waits=[ve], inc=ve)
                attmul_ve[g] = ve.n
                if m == 0 and h + 1 < H:
                    emit_bounds_sq(h + 1)
                if m == 1 and h + 1 < H:
                    emit_bounds(h + 1)
                    bounds_mm[h + 1] = mm.n

            fin_at = {}
            finb_at = {}
            for m in range(4):
                nch = 4 * m + 4
                nhalf = 2 * nch
                for hc in range(nhalf):
                    c = hc // 2
                    half = hc % 2
                    u = c - 4 * m if c > 4 * m else 0
                    rec = {"c": c, "half": half, "u": u, "m": m, "hc": hc,
                           "diag": (c >= 4 * m), "ud": c - 4 * m,
                           "first": (hc == 0), "last": (hc == nhalf - 1)}
                    seq.append(rec)
                    t = gidx
                    gidx += 1
                    emit_ST(t, rec)
                    emit_exp(t, rec)
                    if t >= 2:
                        emit_AV(t - 2, seq[t - 2])
                    if t in finb_at:
                        mf, rv = finb_at.pop(t)
                        emit_finalize_b(mf, rv)
                    if t >= 3:
                        emit_L(t - 3, seq[t - 3])
                        if (t - 3) in fin_at:
                            mf, tE = fin_at.pop(t - 3)
                            finb_at[t + 1] = (mf, emit_finalize_a(mf, tE))
                fin_at[gidx - 1] = (m, gidx - 1)
            emit_AV(gidx - 2, seq[gidx - 2])
            emit_AV(gidx - 1, seq[gidx - 1])
            for tf in (gidx - 3, gidx - 2, gidx - 1):
                emit_L(tf, seq[tf])
                if tf in fin_at:
                    mf, tE = fin_at.pop(tf)
                    emit_finalize_b(mf, emit_finalize_a(mf, tE))
            for k2 in list(finb_at):
                mf, rv = finb_at.pop(k2)
                emit_finalize_b(mf, rv)
            hd_end[h] = (mm.n, se.n)
            if h + 2 < H:
                head_loads(h + 2)
        ph.play()

    if "attsp" in tapset:
        ph = Phase(nc, "ptap")
        dma(ph, "sync", dAtt[:, :], att[:, :], waits=[ve], inc=SX)
        ph.play()

    ph = Phase(nc, "pd")
    fence(ph)
    with ExitStack() as pctx:
        psb = lambda n, s, dt: pctx.enter_context(nc.sbuf_tensor(n, s, dt))
        pps = lambda n, s: pctx.enter_context(nc.psum_tensor(n, s, F32))
        wbig = psb("d_w", [128, 16 * D], BF16)
        po = [pps(f"d_p{j}", [128, 2048]) for j in range(2)]
        xq2 = [psb(f"d_x{j}", [128, D], F32) for j in range(2)]
        cen = [psb(f"d_c{j}", [128, D], F32) for j in range(2)]
        sq = [psb("d_s0", [128, D], BF16)] * 2
        st1 = [psb(f"d_m{j}", [128, 4], F32) for j in range(2)]
        st2 = [psb(f"d_v{j}", [128, 4], F32) for j in range(2)]
        x2f = [psb("d_20", [128, D], F32)] * 2
        ob = [psb(f"d_b{j}", [128, D], BF16) for j in range(2)]
        load_w_strips(ph, dWoT, wbig, waits=[mm])
        se_evac = {}
        ve_x2add = {}
        ve_obmul = {}
        for tt in range(16):
            pp = tt % 2
            dma(ph, "sync", xq2[pp][:, :], dXq[128 * tt:128 * (tt + 1), :],
                waits=([ve] if tt < 2 else []),
                waits_at=([(ve, ve_x2add[tt - 2])] if tt >= 2 else ()), inc=LX)
            for ic in range(16):
                for q in range(4):
                    ph.op("tensor", lambda E, pp=pp, ic=ic, q=q, tt=tt: E.matmul(
                        po[pp][:, 512 * q:512 * (q + 1)],
                        att[:, TOW * ic + 128 * tt:TOW * ic + 128 * (tt + 1)],
                        wbig[:, D * ic + 512 * q:D * ic + 512 * (q + 1)],
                        start=(ic == 0), stop=False),
                        waits=(([LW, ve, se] if tt < 2 else [LW])
                               if (ic == 0 and q == 0) else ()),
                        waits_at=([(se, se_evac[tt - 2])]
                                  if (ic == 0 and q == 0 and tt >= 2) else ()))
            for q in range(4):
                ph.op("tensor", lambda E, pp=pp, q=q: E.matmul(
                    po[pp][:, 512 * q:512 * (q + 1)],
                    ones_row[0:1, :], bo_bf[0:1, 512 * q:512 * (q + 1)],
                    start=False, stop=True),
                    waits=([LW] if q == 0 else ()),
                    inc=(mm if q == 3 else None))
            ph.op("scalar", lambda E, pp=pp: E.activation(cen[pp][:, :], po[pp][:, :],
                                                          AF.Copy),
                  waits=([mm, ve] if tt < 2 else [mm]),
                  waits_at=([(ve, ve_obmul[tt - 2])] if tt >= 2 else ()), inc=se)
            se_evac[tt] = se.n
            ph.op("vector", lambda E, pp=pp: E.reduce_sum(st1[pp][:, 0:1], cen[pp][:, :],
                                                          AX.X), waits=[se], inc=ve)
            ph.op("vector", lambda E, pp=pp: E.tensor_scalar(st1[pp][:, 1:2],
                                                             st1[pp][:, 0:1],
                                                             -1.0 / D, None, ALU.mult),
                  waits=[ve], inc=ve)
            ph.op("scalar", lambda E, pp=pp: E.activation(cen[pp][:, :], cen[pp][:, :],
                                                          AF.Identity,
                                                          bias=st1[pp][:, 1:2]),
                  waits=[ve], inc=se)
            ph.op("scalar", lambda E, pp=pp: E.activation(sq[pp][:, :], cen[pp][:, :],
                                                          AF.Square,
                                                          accum_out=st2[pp][:, 0:1]),
                  waits=[ve, se], inc=se)
            ph.op("vector", lambda E, pp=pp: E.tensor_scalar(st2[pp][:, 1:2],
                                                             st2[pp][:, 0:1],
                                                             1.0 / D, EPS, ALU.mult,
                                                             ALU.add),
                  waits=[se], inc=ve)
            ph.op("scalar", lambda E, pp=pp: E.activation(st2[pp][:, 2:3], st2[pp][:, 1:2],
                                                          AF.Sqrt), waits=[ve], inc=se)
            ph.op("vector", lambda E, pp=pp: E.reciprocal(st2[pp][:, 3:4], st2[pp][:, 2:3]),
                  waits=[se], inc=ve)
            ph.op("vector", lambda E, pp=pp: E.tensor_scalar(cen[pp][:, :], cen[pp][:, :],
                                                             st2[pp][:, 3:4], None,
                                                             ALU.mult),
                  waits=[ve], inc=ve)
            ph.op("vector", lambda E, pp=pp: E.tensor_tensor(x2f[pp][:, :], xq2[pp][:, :],
                                                             cen[pp][:, :], ALU.add),
                  waits=[ve, LX], inc=ve)
            ve_x2add[tt] = ve.n
            ph.op("scalar", lambda E, pp=pp: E.activation(ob[pp][:, :], x2f[pp][:, :],
                                                          AF.Copy),
                  waits=[ve, SX], inc=se)
            dma(ph, "sync", dx2[128 * tt:128 * (tt + 1), :], ob[pp][:, :],
                waits=[se], inc=SX)
            ph.op("vector", lambda E, pp=pp: E.reduce_sum(st1[pp][:, 2:3], x2f[pp][:, :],
                                                          AX.X), waits=[ve], inc=ve)
            ph.op("vector", lambda E, pp=pp: E.tensor_scalar(st1[pp][:, 3:4],
                                                             st1[pp][:, 2:3],
                                                             -1.0 / D, None, ALU.mult),
                  waits=[ve], inc=ve)
            ph.op("scalar", lambda E, pp=pp: E.activation(cen[pp][:, :], x2f[pp][:, :],
                                                          AF.Identity,
                                                          bias=st1[pp][:, 3:4]),
                  waits=[ve], inc=se)
            ph.op("scalar", lambda E, pp=pp: E.activation(sq[pp][:, :], cen[pp][:, :],
                                                          AF.Square,
                                                          accum_out=st2[pp][:, 0:1]),
                  waits=[ve, se], inc=se)
            ph.op("vector", lambda E, pp=pp: E.tensor_scalar(st2[pp][:, 1:2],
                                                             st2[pp][:, 0:1],
                                                             1.0 / D, EPS, ALU.mult,
                                                             ALU.add),
                  waits=[se], inc=ve)
            ph.op("scalar", lambda E, pp=pp: E.activation(st2[pp][:, 2:3], st2[pp][:, 1:2],
                                                          AF.Sqrt), waits=[ve], inc=se)
            ph.op("vector", lambda E, pp=pp: E.reciprocal(st2[pp][:, 3:4], st2[pp][:, 2:3]),
                  waits=[se], inc=ve)
            ph.op("vector", lambda E, pp=pp: E.tensor_scalar(ob[pp][:, :], cen[pp][:, :],
                                                             st2[pp][:, 3:4], None,
                                                             ALU.mult),
                  waits=[ve, se, SX], inc=ve)
            ve_obmul[tt] = ve.n
            dma(ph, "sync", dh2[128 * tt:128 * (tt + 1), :], ob[pp][:, :],
                waits=[ve], inc=SX)
        ph.play()
    att_ctx.close()

    ph = Phase(nc, "pe")
    fence(ph)
    with ExitStack() as pctx:
        psb = lambda n, s, dt: pctx.enter_context(nc.sbuf_tensor(n, s, dt))
        pps = lambda n, s: pctx.enter_context(nc.psum_tensor(n, s, F32))
        h2f = psb("e_h2", [128, 16 * 1024], BF16)
        gw = psb("e_gw", [128, H * NE], BF16)
        lgbf = psb("e_lg", [8, 1024], BF16)
        ltm = psb("e_ltm", [128, 64], F32)
        nmx = psb("e_nmx", [128, 1], F32)
        prb = psb("e_prb", [128, 8], F32)
        ssum = psb("e_ss", [128, 1], F32)
        rs = psb("e_rs", [128, 1], F32)
        gsv = psb("e_gs", [128, 4], F32)
        g1 = psb("e_g1", [128, 1], F32)
        gk1 = psb("e_gk1", [128, 4], F32)
        gs2 = psb("e_gs2", [128, 4], F32)
        g2 = psb("e_g2", [128, 1], F32)
        gk2 = psb("e_gk2", [128, 4], F32)
        smask = psb("e_sm", [128, 8], F32)
        mskd = psb("e_mkd", [128, 8], F32)
        m1 = psb("e_m1", [128, 1], F32)
        mk1 = psb("e_mk1", [128, 8], F32)
        ms2 = psb("e_ms2", [128, 8], F32)
        m2 = psb("e_m2", [128, 1], F32)
        mk2 = psb("e_mk2", [128, 8], F32)
        wm = psb("e_wm", [128, 64], F32)
        ybuf = psb("e_y", [128, 22 * 1024], BF16)
        wst = [psb(f"e_ws{j}", [128, 2048], BF16) for j in range(6)]
        wd = psb("e_wd", [128, 11264], BF16)
        racc = psb("e_racc", [128, 8 * D], BF16)
        sil = psb("e_sil", [128, 1024], BF16)
        xr = [psb(f"e_x{j}", [128, D], BF16) for j in range(2)]
        cen = [psb(f"e_c{j}", [128, D], F32) for j in range(2)]
        sq = [psb("e_s0", [128, D], BF16)] * 2
        st1 = [psb(f"e_st1{j}", [128, 4], F32) for j in range(2)]
        st2 = [psb(f"e_st2{j}", [128, 4], F32) for j in range(2)]
        yout = [psb("e_yo0", [128, D], F32)] * 2
        pg = pps("e_pg", [128, 1024])
        pu = pps("e_pu", [128, 1024])
        p2 = [pps(f"e_p2{j}", [128, 1024]) for j in range(2)]
        plg = p2[0][0:8, :]
        plt = pg[:, 0:8]

        dma(ph, "gpsimd", gw[:, :].rearrange("p (ic e) -> p ic e", e=NE),
            dGWT.rearrange("(ic p) e -> p ic e", p=128), waits=[mm], inc=LW)

        def h2f_loads(hf2, mm_at=None):
            trow2 = 1024 * hf2
            gend = []
            for g2 in range(4):
                for ic in range(4 * g2, 4 * g2 + 4):
                    dma(ph, "sync", h2f[:, 1024 * ic:1024 * (ic + 1)],
                        dh2[trow2:trow2 + 1024, 128 * ic:128 * (ic + 1)],
                        waits=([SX, mm] if mm_at is None else [SX]),
                        waits_at=([(mm, mm_at)] if mm_at is not None else ()),
                        inc=TQ[g2], transpose=True)
                gend.append((TQ[g2], TQ[g2].n))
            return [gend[ic // 4] for ic in range(16)]

        prev_adds_ve = None
        prev_se_end = None
        prev_slot = None
        def emit_final_ln(hf3):
            for tt in range(8):
                pp = tt % 2
                gt = 8 * hf3 + tt
                dma(ph, "sync", xr[pp][:, :], dx2[128 * gt:128 * (gt + 1), :],
                    waits=[ve], inc=LX)
                rt = racc[:, D * tt:D * (tt + 1)]
                ph.op("vector", lambda E, pp=pp, rt=rt: E.reduce_sum(st1[pp][:, 0:1], rt,
                                                                     AX.X),
                      waits=[ve], inc=ve)
                ph.op("vector", lambda E, pp=pp: E.tensor_scalar(st1[pp][:, 1:2],
                                                                 st1[pp][:, 0:1],
                                                                 1.0 / D, None, ALU.mult),
                      waits=[ve], inc=ve)
                ph.op("vector", lambda E, pp=pp, rt=rt: E.tensor_scalar(cen[pp][:, :], rt,
                                                                        st1[pp][:, 1:2],
                                                                        None, ALU.subtract),
                      waits=[ve, se], inc=ve)
                ph.op("scalar", lambda E, pp=pp: E.activation(sq[pp][:, :], cen[pp][:, :],
                                                              AF.Square,
                                                              accum_out=st2[pp][:, 0:1]),
                      waits=[ve, se], inc=se)
                ph.op("vector", lambda E, pp=pp: E.tensor_scalar(st2[pp][:, 1:2],
                                                                 st2[pp][:, 0:1],
                                                                 1.0 / D, EPS, ALU.mult,
                                                                 ALU.add),
                      waits=[se], inc=ve)
                ph.op("scalar", lambda E, pp=pp: E.activation(st2[pp][:, 2:3],
                                                              st2[pp][:, 1:2],
                                                              AF.Sqrt),
                      waits=[ve], inc=se)
                ph.op("vector", lambda E, pp=pp: E.reciprocal(st2[pp][:, 3:4],
                                                              st2[pp][:, 2:3]),
                      waits=[se], inc=ve)
                ph.op("vector", lambda E, pp=pp: E.tensor_scalar(cen[pp][:, :],
                                                                 cen[pp][:, :],
                                                                 st2[pp][:, 3:4], None,
                                                                 ALU.mult),
                      waits=[ve], inc=ve)
                ph.op("vector", lambda E, pp=pp: E.tensor_tensor(yout[pp][:, :],
                                                                 cen[pp][:, :],
                                                                 xr[pp][:, :], ALU.add),
                      waits=[ve, LX, SY], inc=ve)
                dma(ph, "sync", dY[128 * gt:128 * (gt + 1), :], yout[pp][:, :],
                    waits=[ve], inc=SY)

        lh_h2f_next = None
        for hf in range(2):
            trow = 1024 * hf
            if hf == 0:
                lh_h2f = h2f_loads(0)
            else:
                lh_h2f = lh_h2f_next
            for ic in range(16):
                for q in range(2):
                    w_at = []
                    if q == 0:
                        w_at.append(lh_h2f[ic])
                    if ic == 0 and q == 0 and hf == 1:
                        w_at.append((ve, prev_adds_ve))
                    ph.op("tensor", lambda E, ic=ic, q=q: E.matmul(
                        plg[:, 512 * q:512 * (q + 1)], gw[:, NE * ic:NE * (ic + 1)],
                        h2f[:, 1024 * ic + 512 * q:1024 * ic + 512 * (q + 1)],
                        start=(ic == 0), stop=(ic == 15)),
                        waits=(([LW, ve] if hf == 0 else [LW])
                               if (ic == 0 and q == 0) else ()),
                        waits_at=w_at,
                        inc=(mm if (ic == 15 and q == 1) else None))
            ph.op("scalar", lambda E: E.activation(lgbf[:, :], plg, AF.Copy),
                  waits=[mm], inc=se)
            lt_g = se.n
            for tt in range(8):
                ph.op("tensor", lambda E, tt=tt: E.matmul(
                    plt, lgbf[0:8, 128 * tt:128 * (tt + 1)], eye8[0:8, 0:8],
                    start=True, stop=True),
                    waits_at=[(se, lt_g)], inc=mm)
                ph.op("scalar", lambda E, tt=tt: E.activation(
                    ltm[:, 8 * tt:8 * (tt + 1)], plt, AF.Copy),
                    waits=[mm], inc=se)
                lt_g = se.n
            for tt in range(8):
                lt = ltm[:, 8 * tt:8 * (tt + 1)]
                ph.op("vector", lambda E, lt=lt: E.tensor_reduce(nmx[:, :], lt, AX.X,
                                                                 ALU.max, negate=True),
                      waits=[ve, se], inc=ve)
                ph.op("scalar", lambda E, lt=lt: E.activation(prb[:, :], lt, AF.Exp,
                                                              bias=nmx[:, 0:1],
                                                              accum_out=ssum[:, 0:1]),
                      waits=[ve, se], inc=se)
                ph.op("vector", lambda E: E.reciprocal(rs[:, :], ssum[:, :]),
                      waits=[se], inc=ve)
                ph.op("vector", lambda E: E.tensor_scalar(prb[:, :], prb[:, :],
                                                          rs[:, 0:1], None, ALU.mult),
                      waits=[ve], inc=ve)
                pe_ = prb[:, :].rearrange("p (g e) -> p g e", e=2)
                ph.op("vector", lambda E, pe_=pe_: E.tensor_tensor(
                    gsv[:, :], pe_[:, :, 0], pe_[:, :, 1], ALU.max),
                    waits=[ve], inc=ve)
                ph.op("vector", lambda E: E.tensor_reduce(g1[:, :], gsv[:, :], AX.X,
                                                          ALU.max), waits=[ve], inc=ve)
                ph.op("vector", lambda E: E.tensor_scalar(gk1[:, :], gsv[:, :],
                                                          g1[:, 0:1], None, ALU.is_ge),
                      waits=[ve], inc=ve)
                ph.op("vector", lambda E: E.tensor_scalar(gs2[:, :], gk1[:, :], -1.0, 1.0,
                                                          ALU.mult, ALU.add),
                      waits=[ve], inc=ve)
                ph.op("vector", lambda E: E.tensor_tensor(gs2[:, :], gs2[:, :], gsv[:, :],
                                                          ALU.mult), waits=[ve], inc=ve)
                ph.op("vector", lambda E: E.tensor_reduce(g2[:, :], gs2[:, :], AX.X,
                                                          ALU.max), waits=[ve], inc=ve)
                ph.op("vector", lambda E: E.tensor_scalar(gk2[:, :], gs2[:, :],
                                                          g2[:, 0:1], None, ALU.is_ge),
                      waits=[ve], inc=ve)
                ph.op("vector", lambda E: E.tensor_tensor(gk2[:, :], gk2[:, :], gk1[:, :],
                                                          ALU.max), waits=[ve], inc=ve)
                sm_ = smask[:, :].rearrange("p (g e) -> p g e", e=2)
                ph.op("vector", lambda E, sm_=sm_: E.tensor_copy(sm_[:, :, 0], gk2[:, :]),
                      waits=[ve], inc=ve)
                ph.op("vector", lambda E, sm_=sm_: E.tensor_copy(sm_[:, :, 1], gk2[:, :]),
                      waits=[ve], inc=ve)
                ph.op("vector", lambda E: E.tensor_tensor(mskd[:, :], prb[:, :],
                                                          smask[:, :], ALU.mult),
                      waits=[ve], inc=ve)
                ph.op("vector", lambda E: E.tensor_reduce(m1[:, :], mskd[:, :], AX.X,
                                                          ALU.max), waits=[ve], inc=ve)
                ph.op("vector", lambda E: E.tensor_scalar(mk1[:, :], mskd[:, :],
                                                          m1[:, 0:1], None, ALU.is_ge),
                      waits=[ve], inc=ve)
                ph.op("vector", lambda E: E.tensor_scalar(ms2[:, :], mk1[:, :], -1.0, 1.0,
                                                          ALU.mult, ALU.add),
                      waits=[ve], inc=ve)
                ph.op("vector", lambda E: E.tensor_tensor(ms2[:, :], ms2[:, :], mskd[:, :],
                                                          ALU.mult), waits=[ve], inc=ve)
                ph.op("vector", lambda E: E.tensor_reduce(m2[:, :], ms2[:, :], AX.X,
                                                          ALU.max), waits=[ve], inc=ve)
                ph.op("vector", lambda E: E.tensor_scalar(mk2[:, :], ms2[:, :],
                                                          m2[:, 0:1], None, ALU.is_ge),
                      waits=[ve], inc=ve)
                ph.op("vector", lambda E: E.tensor_tensor(mk2[:, :], mk2[:, :], mk1[:, :],
                                                          ALU.max), waits=[ve], inc=ve)
                ph.op("vector", lambda E, tt=tt: E.tensor_tensor(
                    wm[:, 8 * tt:8 * (tt + 1)], mskd[:, :], mk2[:, :], ALU.mult),
                    waits=[ve], inc=ve)
            if hf == 1:
                emit_final_ln(0)

            def ws_load(fc, ga, ua, mm_at=None):
                ws0, ws1 = wst[2 * (fc % 3)], wst[2 * (fc % 3) + 1]
                w, wa = ([mm], ()) if mm_at is None else ([], [(mm, mm_at)])
                dma(ph, "gpsimd", ws0[:, :], ga[fc], waits=w, waits_at=wa, inc=LW)
                dma(ph, "gpsimd", ws1[:, :], ua[fc], waits=w, waits_at=wa, inc=LW)

            def stage1(nfc, ga, ua, ent=None, pf_at=None, lh_ic=None):
                pgp = [pg, p2[0]]
                pup = [pu, p2[1]]
                ent_se, ent_ve = ent if ent is not None else (se.n, ve.n)
                s1se, s1ve = {}, {}
                slot_mm = {}
                lw_after = {}
                for pf in range(3):
                    if pf < nfc:
                        ws_load(pf, ga, ua,
                                None if pf_at is None else pf_at.get(pf % 3))
                        lw_after[pf] = LW.n
                for fc in range(nfc):
                    ws0, ws1 = wst[2 * (fc % 3)], wst[2 * (fc % 3) + 1]
                    pgc, puc = pgp[fc % 2], pup[fc % 2]
                    w_at = [(se, s1se.get(fc - 2, ent_se)),
                            (ve, s1ve.get(fc - 2, ent_ve)),
                            (LW, lw_after[fc])]
                    for ic in range(16):
                        for q in range(2):
                            w2 = list(w_at) if (ic == 0 and q == 0) else []
                            if fc == 0 and q == 0 and lh_ic is not None:
                                w2.append(lh_ic[ic])
                            ph.op("tensor", lambda E, ws0=ws0, ic=ic, q=q, pgc=pgc: E.matmul(
                                pgc[:, 512 * q:512 * (q + 1)],
                                ws0[:, 128 * ic:128 * (ic + 1)],
                                h2f[:, 1024 * ic + 512 * q:1024 * ic + 512 * (q + 1)],
                                start=(ic == 0), stop=(ic == 15)),
                                waits=(([LH] if lh_ic is None else [])
                                       if (ic == 0 and q == 0) else ()),
                                waits_at=w2,
                                inc=(mm if (ic == 15 and q == 1) else None))
                    for ic in range(16):
                        for q in range(2):
                            ph.op("tensor", lambda E, ws1=ws1, ic=ic, q=q, puc=puc: E.matmul(
                                puc[:, 512 * q:512 * (q + 1)],
                                ws1[:, 128 * ic:128 * (ic + 1)],
                                h2f[:, 1024 * ic + 512 * q:1024 * ic + 512 * (q + 1)],
                                start=(ic == 0), stop=(ic == 15)),
                                inc=(mm if (ic == 15 and q == 1) else None))
                    slot_mm[fc % 3] = mm.n
                    if fc + 3 < nfc:
                        ws_load(fc + 3, ga, ua)
                        lw_after[fc + 3] = LW.n
                    ph.op("scalar", lambda E, pgc=pgc: E.activation(sil[:, :], pgc[:, :],
                                                                    AF.Silu),
                          waits=[mm, ve, se], inc=se)
                    s1se[fc] = se.n
                    ph.op("vector", lambda E, fc=fc, puc=puc: E.tensor_tensor(
                        ybuf[:, 1024 * fc:1024 * (fc + 1)], puc[:, :], sil[:, :],
                        ALU.mult),
                        waits=[se], inc=ve)
                    s1ve[fc] = ve.n
                return slot_mm
            for fc in range(22):
                dma(ph, "scalar", wd[:, 512 * fc:512 * (fc + 1)],
                    dWsdT[128 * fc:128 * (fc + 1), 0:512],
                    waits=[mm], inc=LW2)
            lw2_sh = LW2.n
            last_slot = stage1(22, dWsgT, dWsuT,
                               ent=(None if hf == 0
                                    else (prev_se_end, prev_adds_ve)),
                               pf_at=(prev_slot if hf == 1 else {0: 0, 1: 0, 2: 0}),
                               lh_ic=lh_h2f)
            shc_ve = {}
            lw2A = {0: lw2_sh}
            lw2B = {0: lw2_sh}
            for oh in range(4):
                a_done = None
                for tt in range(8):
                    idx = 8 * oh + tt
                    pp = tt % 2
                    for fc in range(22):
                        w_at = []
                        if fc == 0:
                            w_at.append((LW2, lw2A[oh]))
                            if idx >= 2:
                                w_at.append((ve, shc_ve[idx - 2]))
                        if fc == 11 and tt == 0:
                            w_at.append((LW2, lw2B[oh]))
                        ph.op("tensor", lambda E, pp=pp, fc=fc, tt=tt: E.matmul(
                            p2[pp][:, 0:512],
                            ybuf[:, 1024 * fc + 128 * tt:1024 * fc + 128 * (tt + 1)],
                            wd[:, 512 * fc:512 * (fc + 1)],
                            start=(fc == 0), stop=(fc == 21)),
                            waits=([ve] if (fc == 0 and idx < 2) else ()),
                            waits_at=w_at,
                            inc=(mm if fc in (10, 21) else None))
                        if fc == 10 and tt == 7:
                            a_done = mm.n
                    ph.op("vector", lambda E, pp=pp, tt=tt, oh=oh: E.tensor_copy(
                        racc[:, D * tt + 512 * oh:D * tt + 512 * (oh + 1)],
                        p2[pp][:, 0:512]),
                        waits=[mm], inc=ve)
                    shc_ve[idx] = ve.n
                if oh < 3:
                    for fc in range(11):
                        dma(ph, "scalar", wd[:, 512 * fc:512 * (fc + 1)],
                            dWsdT[128 * fc:128 * (fc + 1),
                                  512 * (oh + 1):512 * (oh + 2)],
                            waits_at=[(mm, a_done)], inc=LW2)
                    lw2A[oh + 1] = LW2.n
                    b_at = mm.n
                    for fc in range(11, 22):
                        dma(ph, "scalar", wd[:, 512 * fc:512 * (fc + 1)],
                            dWsdT[128 * fc:128 * (fc + 1),
                                  512 * (oh + 1):512 * (oh + 2)],
                            waits_at=[(mm, b_at)], inc=LW2)
                    lw2B[oh + 1] = LW2.n
            wdb = [wd, ybuf[:, 11264:22528]]
            lw2_at = {}
            mm_after = {}

            def wd_load(k, wat):
                n2, oh2 = divmod(k, 2)
                for fc in range(11):
                    dma(ph, "scalar", wdb[k % 2][:, 1024 * fc:1024 * (fc + 1)],
                        dWdT[n2, 128 * fc:128 * (fc + 1),
                             1024 * oh2:1024 * (oh2 + 1)],
                        waits=([mm] if wat is None else []),
                        waits_at=([(mm, wat)] if wat is not None else ()),
                        inc=LW2)
                lw2_at[k] = LW2.n
            wd_load(0, None)
            wd_load(1, None)
            add_ve = {}
            for n in range(NE):
                last_slot = stage1(11, dWgT[n], dWuT[n], pf_at=last_slot,
                                   lh_ic=lh_h2f)
                if n == NE - 1 and hf == 0:
                    lh_h2f_next = h2f_loads(1, mm_at=mm.n)
                for oh in range(2):
                    k = 2 * n + oh
                    s2_se, s2_ve = se.n, ve.n
                    for tt in range(8):
                        pp = tt % 2
                        w_at = []
                        if tt == 0:
                            w_at.append((LW2, lw2_at[k]))
                        if tt >= 2:
                            w_at.append((ve, add_ve[(k, tt - 2)]))
                        elif oh == 0:
                            w_at.append((ve, s2_ve))
                            if tt == 0:
                                w_at.append((se, s2_se))
                        else:
                            w_at.append((ve, add_ve[(k - 1, 6 + tt)]))
                        for fc in range(11):
                            for q in range(2):
                                ph.op("tensor", lambda E, pp=pp, fc=fc, q=q, tt=tt,
                                      k=k: E.matmul(
                                    p2[pp][:, 512 * q:512 * (q + 1)],
                                    ybuf[:, 1024 * fc + 128 * tt:1024 * fc + 128 * (tt + 1)],
                                    wdb[k % 2][:, 1024 * fc + 512 * q:1024 * fc + 512 * (q + 1)],
                                    start=(fc == 0), stop=(fc == 10)),
                                    waits_at=(w_at if (fc == 0 and q == 0) else ()),
                                    inc=(mm if (fc == 10 and q == 1) else None))
                        for q in range(2):
                            ph.op("vector", lambda E, pp=pp, tt=tt, q=q, n=n,
                                  oh=oh: E.scalar_tensor_tensor(
                                racc[:, D * tt + 1024 * oh + 512 * q:
                                     D * tt + 1024 * oh + 512 * (q + 1)],
                                p2[pp][:, 512 * q:512 * (q + 1)],
                                wm[:, 8 * tt + n:8 * tt + n + 1],
                                racc[:, D * tt + 1024 * oh + 512 * q:
                                     D * tt + 1024 * oh + 512 * (q + 1)],
                                ALU.mult, ALU.add),
                                waits=[mm], inc=ve)
                        add_ve[(k, tt)] = ve.n
                    mm_after[k] = mm.n
                    if k + 2 < 2 * NE:
                        wd_load(k + 2, mm_after[k])
            prev_adds_ve = add_ve[(2 * NE - 1, 7)]
            prev_se_end = se.n
            prev_slot = last_slot
            if hf == 1:
                emit_final_ln(1)

        ph.op("sync", lambda E: E.nop(), waits=[SY])
        ph.play()

    ctx.close()
    return nc


def _mk_mask(r):
    mk = np.zeros((128, 4 * 128), np.float32)
    for kk in range(4):
        if kk < r:
            mk[:, 128 * kk:128 * (kk + 1)] = 1.0
        elif kk == r:
            mk[:, 128 * kk:128 * (kk + 1)] = np.triu(np.ones((128, 128), np.float32))
    return mk


def kernel(L0_attn_X, L0_attn_Wq, L0_attn_bq, L0_attn_Wk, L0_attn_bk,
           L0_attn_Wv, L0_attn_bv, L0_attn_Wo, L0_attn_bo,
           L0_ffn_gate_weight, L0_ffn_gate_w, L0_ffn_up_w, L0_ffn_down_w,
           L0_ffn_sh_gate_w, L0_ffn_sh_up_w, L0_ffn_sh_down_w):
    X = np.ascontiguousarray(np.asarray(L0_attn_X, dtype=np.float32))
    tr = lambda a: np.ascontiguousarray(np.asarray(a, np.float32).T)
    row = lambda a: np.ascontiguousarray(np.asarray(a, np.float32).reshape(1, -1))

    def swz(a):
        F_, D_ = a.shape
        return np.ascontiguousarray(
            a.reshape(F_ // 128, 128, D_ // 128, 128).transpose(0, 3, 2, 1)
            .reshape(F_ // 128, 128, D_))
    BF = ml_dtypes.bfloat16
    bf = lambda a: np.ascontiguousarray(a.astype(BF))
    WqT, WkT, WvT, WoT = (bf(tr(L0_attn_Wq)), bf(tr(L0_attn_Wk)),
                          bf(tr(L0_attn_Wv)), bf(tr(L0_attn_Wo)))
    GWT = bf(tr(L0_ffn_gate_weight))
    gw_ = np.asarray(L0_ffn_gate_w, np.float32)
    up_ = np.asarray(L0_ffn_up_w, np.float32)
    WgT = bf(np.stack([swz(gw_[n]) for n in range(NE)]))
    WuT = bf(np.stack([swz(up_[n]) for n in range(NE)]))
    WdT = bf(np.asarray(L0_ffn_down_w, np.float32).transpose(0, 2, 1))
    WsgT = bf(swz(np.asarray(L0_ffn_sh_gate_w, np.float32)))
    WsuT = bf(swz(np.asarray(L0_ffn_sh_up_w, np.float32)))
    WsdT = bf(tr(L0_ffn_sh_down_w))

    nc = build_program()
    in_maps = []
    for c in range(8):
        b, r = c // 4, c % 4
        Xb = X[b]
        tiles = Xb.reshape(64, 128, D)
        Xq = np.ascontiguousarray(tiles[[4 * j + r for j in range(16)]]).reshape(TOW, D)
        in_maps.append({
            "Xq": Xq, "Xkv": Xb,
            "WqT": WqT, "WkT": WkT, "WvT": WvT, "WoT": WoT,
            "bq": row(L0_attn_bq), "bk": row(L0_attn_bk),
            "bv": row(L0_attn_bv), "bo": row(L0_attn_bo),
            "GWT": GWT, "WgT": WgT, "WuT": WuT, "WdT": WdT,
            "WsgT": WsgT, "WsuT": WsuT, "WsdT": WsdT,
            "MK01": _mk_mask(r),
        })
    res = run_bass_kernel_spmd(nc, in_maps, core_ids=list(range(8)))
    Y = np.empty((2, SBT, D), np.float32)
    for c in range(8):
        b, r = c // 4, c % 4
        yo = np.asarray(res.results[c]["Yo"]).reshape(16, 128, D)
        for j in range(16):
            p = 4 * j + r
            Y[b, 128 * p:128 * (p + 1), :] = yo[j]
    return Y
